# version 12
# speedup vs baseline: 1.3371x; 1.0104x over previous
import numpy as np
import concourse.bass as bass
import concourse.mybir as mybir
from concourse.bass import broadcast_tensor_aps
from concourse.bass_utils import run_bass_kernel_spmd

F32 = mybir.dt.float32
BF16 = mybir.dt.bfloat16
I32 = mybir.dt.int32
AX = mybir.AxisListType
AF = mybir.ActivationFunctionType
ALU = mybir.AluOpType

S, D, H, KD, HD, DH = 2048, 1024, 16, 64, 128, 4096
NT = S // 128
CH = 128
NC = S // CH
ENGINES = ("tensor", "vector", "scalar", "gpsimd", "sync")
GRPS = [(0, 1, 2), (3, 4, 5), (6, 7, 8), (9, 10, 11), (12, 13, 14), (15,)]


class Ctr:
    def __init__(self, handle):
        self.h = handle
        self.n = 0

    def tick(self, k=1):
        self.n += k
        return self.n


class Prog:
    def __init__(self, nc):
        self.nc = nc
        self.streams = {e: [] for e in ENGINES}

    def op(self, engine, f, after=(), inc=None, inc_k=1):
        after = [t for t in after if t is not None]
        after = [(c, v) for (c, v) in after if v > 0]
        for c, v in after:
            assert v <= c.n, f"wait on future sem value {v} > {c.n}"
        val = inc.tick(inc_k) if inc is not None else None
        semh = inc.h if inc is not None else None

        def emit(eng, after=after, f=f, semh=semh, inc_k=inc_k):
            for c, v in after:
                eng.wait_ge(c.h, v)
            ins = f(eng)
            if semh is not None:
                ins.then_inc(semh, inc_k)
        self.streams[engine].append(emit)
        return val

    def wait(self, engine, after):
        after = [t for t in after if t is not None]
        after = [(c, v) for (c, v) in after if v > 0]
        for c, v in after:
            assert v <= c.n

        def emit(eng, after=after):
            for c, v in after:
                eng.wait_ge(c.h, v)
        self.streams[engine].append(emit)


def host_constants():
    import ml_dtypes
    bf = ml_dtypes.bfloat16
    angle = (1.0 / 10000.0 ** np.linspace(0.0, 1.0, KD // 2)).astype(np.float64)
    angle = np.repeat(angle, 2)
    idx = np.arange(S, dtype=np.float64)
    sin = np.sin(idx[:, None] * angle[None, :])
    cos = np.cos(idx[:, None] * angle[None, :])
    sin_fm = np.ascontiguousarray(np.tile(sin.T, (2, 1))).astype(np.float32)
    cos_fm = np.ascontiguousarray(np.tile(cos.T, (2, 1))).astype(np.float32)
    PT = np.zeros((128, 128), np.float32)
    for m in range(128):
        if m % 2 == 0:
            PT[m + 1, m] = -1.0
        else:
            PT[m - 1, m] = 1.0
    PT = PT.astype(bf)
    decay = np.log(1.0 - 2.0 ** (-5.0 - np.arange(H, dtype=np.float64)))
    lam = np.exp(decay)
    t_i = np.arange(CH)
    diff = t_i[None, :] - t_i[:, None]
    DT = np.where(diff >= 0, np.exp(decay[:, None, None] * diff[None]), 0.0) * 0.125
    DT_sb = np.ascontiguousarray(DT.transpose(1, 0, 2).reshape(CH, H * CH)).astype(np.float32)
    tq = np.arange(CH, dtype=np.float64) + 1.0
    lamq = np.zeros((128, 8 * CH), np.float64)
    for hp in range(8):
        lamq[0:64, hp * CH:(hp + 1) * CH] = lam[2 * hp] ** tq[None, :]
        lamq[64:128, hp * CH:(hp + 1) * CH] = lam[2 * hp + 1] ** tq[None, :]
    lamq = lamq.astype(bf)
    lamk = np.zeros((128, D), np.float64)
    tk = np.arange(CH, dtype=np.float64)
    for h in range(H):
        lamk[:, h * KD:(h + 1) * KD] = (0.125 * lam[h] ** (127.0 - tk))[:, None]
    lamk = lamk.astype(bf)
    lam128 = np.zeros((128, 8), np.float64)
    for hp in range(8):
        lam128[0:64, hp] = lam[2 * hp] ** 128.0
        lam128[64:128, hp] = lam[2 * hp + 1] ** 128.0
    lam128 = lam128.astype(np.float32)
    s_i = np.arange(S, dtype=np.float64)
    Z = (1.0 - lam[None, :] ** (s_i[:, None] + 1.0)) / (1.0 - lam[None, :])
    sqrtZ_sb = np.ascontiguousarray(np.sqrt(Z).reshape(NC, 128, H).transpose(1, 0, 2)
                                    .reshape(128, NC * H)).astype(np.float32)
    ident = np.eye(128, dtype=bf)
    return dict(c_sin=np.asarray(sin_fm), c_cos=np.asarray(cos_fm), c_PT=np.asarray(PT),
                c_DT=np.asarray(DT_sb), c_lamq=np.asarray(lamq), c_lamk=np.asarray(lamk),
                c_lam128=lam128, c_sqrtZ=sqrtZ_sb, c_ident=np.asarray(ident))


def build(parts=("attn_a", "ffn1", "attn_b", "ffn2"), dbg=False):
    import contextlib
    nc = bass.Bass("TRN2", target_bir_lowering=False)
    d = {}

    def din(name, shape, dtype=F32):
        d[name] = nc.dram_tensor(name, list(shape), dtype, kind="ExternalInput").ap()
        return d[name]

    def dscr(name, shape, dtype):
        d[name] = nc.dram_tensor(name, list(shape), dtype,
                                 kind="ExternalOutput" if dbg else "Internal").ap()
        return d[name]

    din("x", (S, D))
    din("wq", (D, D)); din("wk", (D, D)); din("wv", (D, 2 * D)); din("wg", (D, 2 * D))
    din("wo", (2 * D, D))
    din("bq_f", (128, 8)); din("bk_f", (128, 8))
    din("bv_r", (1, 2 * D)); din("bg_r", (1, 2 * D)); din("bo_r", (1, D))
    din("w_inT", (D, 2 * D)); din("w_sT", (D, D)); din("w_oT", (D, D))
    din("bin_f", (128, 16)); din("bs_f", (128, 8)); din("bo2_r", (1, D))
    din("w1T", (D, DH)); din("w2T", (DH, D))
    din("convw", (128, 96)); din("convb", (128, 32))
    din("c_sin", (128, S)); din("c_cos", (128, S))
    din("c_PT", (128, 128), BF16); din("c_DT", (128, H * CH))
    din("c_lamq", (128, 8 * CH), BF16); din("c_lamk", (128, D), BF16)
    din("c_lam128", (128, 8)); din("c_sqrtZ", (128, NC * H))
    din("c_ident", (128, 128), BF16)
    d["out"] = nc.dram_tensor("out", [S, D], F32, kind="ExternalOutput").ap()
    dscr("h_xfm", (128, 8 * S), BF16)
    dscr("h_qr", (128, 8 * S), BF16)
    dscr("h_kr", (128, 8 * S), BF16)
    dscr("h_v", (128, NT * 2 * D), BF16)
    dscr("h_g", (128, NT * 2 * D), BF16)
    dscr("h_w1q", (128, 8 * DH), BF16)
    dscr("h_w2q", (128, 32 * D), BF16)
    dscr("h_hs", (128, 32 * S), BF16)
    dscr("h_ab", (128, 16 * S // 2), F32)
    if dbg:
        dscr("h_st", (128, NC * 8 * 130), F32)
        dscr("h_mix", (128, NC * 2 * D), BF16)
        dscr("h_epi", (128, NC * 128), F32)
        dscr("h_sco", (128, NC * 2 * CH), BF16)
        dscr("h_hq", (128, 32 * 2048), BF16)
        for _i in range(4):
            dscr(f"h_R{_i}", (128, NT * D), F32)

    stack = contextlib.ExitStack()
    with stack:
        def sbuf(name, shape, dtype):
            return stack.enter_context(nc.sbuf_tensor(name, list(shape), dtype))

        def psum_t(name, shape, dtype=F32):
            return stack.enter_context(nc.psum_tensor(name, list(shape), dtype))

        R = sbuf("R", (128, NT * D), F32)
        wbuf = sbuf("wbuf", (128, 8 * 2 * D), BF16)
        AR1 = sbuf("AR1", (128, 16384), BF16)
        AR3 = sbuf("AR3", (128, 12288), BF16)
        ARF = sbuf("ARF", (128, 2048), F32)
        stg = sbuf("stg", (128, 2 * D), BF16)
        Sf = sbuf("Sf", (128, 8 * 130), F32)
        Sb = sbuf("Sb", (128, 8 * 130), BF16)
        junk = sbuf("junk", (128, 128), BF16)
        stat = sbuf("stat", (128, 64), F32)
        epi = sbuf("epi", (128, 16 * 8), F32)
        cons = sbuf("cons", (128, 8), F32)
        onesb = sbuf("onesb", (1, 128), BF16)
        onesc = sbuf("onesc", (128, 1), BF16)
        onescf = sbuf("onescf", (128, 1), F32)
        onesrf = sbuf("onesrf", (1, 128), F32)
        srowb = sbuf("srowb", (1, 2048), BF16)
        scs = sbuf("scs", (128, 1024), F32)
        wsc = sbuf("wsc", (128, 8), F32)
        rowb = sbuf("rowb", (1, 2 * D), BF16)
        rowb2 = sbuf("rowb2", (1, D), BF16)
        PTb = sbuf("PTb", (128, 128), BF16)
        DTb = sbuf("DTb", (128, H * CH), F32)
        lamqb = sbuf("lamqb", (128, 8 * CH), BF16)
        lamkb = sbuf("lamkb", (128, D), BF16)
        lam128b = sbuf("lam128b", (128, 8), F32)
        sqrtZb = sbuf("sqrtZb", (128, NC * H), F32)
        identb = sbuf("identb", (128, 128), BF16)
        bq_fb = sbuf("bq_fb", (128, 8), F32)
        bk_fb = sbuf("bk_fb", (128, 8), F32)
        bin_fb = sbuf("bin_fb", (128, 16), F32)
        nbin_fb = sbuf("nbin_fb", (128, 16), F32)
        bs_fb = sbuf("bs_fb", (128, 8), F32)
        convwb = sbuf("convwb", (128, 96), F32)
        convwsc = sbuf("convwsc", (128, 96), F32)
        convbb = sbuf("convbb", (128, 32), F32)
        ffs = sbuf("ffs", (128, NT * 4), F32)
        lnst = sbuf("lnst", (128, NT * 2), F32)
        i32s = sbuf("i32s", (128, 1024), I32)
        sbB = sbuf("sbB", (128, 64), F32)
        pb = [psum_t(f"pb{i}", (128, 512), F32) for i in range(6)]
        pt_b = [psum_t("pt0", (128, 1024), BF16), psum_t("pt1", (128, 1024), BF16)]

        sems = {}
        for nm in ("t", "v", "s", "g"):
            sems[nm] = Ctr(stack.enter_context(nc.semaphore("sem_" + nm)))
        dpool = {"sync": [Ctr(stack.enter_context(nc.semaphore(f"sem_ds{i}")))
                          for i in range(8)],
                 "gpsimd": [Ctr(stack.enter_context(nc.semaphore(f"sem_dg{i}")))
                            for i in range(4)]}
        didx = {"sync": 0, "gpsimd": 0}
        blk = stack.enter_context(nc.Block())
        p = Prog(nc)
        sT, sV, sS, sG = (sems[k] for k in ("t", "v", "s", "g"))

        def dma(queue, dst, src, after=()):
            c = dpool[queue][didx[queue] % len(dpool[queue])]
            didx[queue] += 1
            v = p.op(queue, lambda e: e.dma_start(out=dst, in_=src), after=after,
                     inc=c, inc_k=16)
            return (c, v)

        def dall(queue):
            return [(c, c.n) for c in dpool[queue]]

        def mm(out, lhsT, rhs, start, stop, after=(), tp=None):
            if tp is None:
                return p.op("tensor", lambda e: e.matmul(out, lhsT, rhs, start=start,
                                                         stop=stop), after=after, inc=sT)
            return p.op("tensor", lambda e: e.matmul(out, lhsT, rhs, start=start,
                                                     stop=stop, tile_position=tp),
                        after=after, inc=sT)

        def trans(out, in_, after=()):
            return p.op("tensor", lambda e: e.transpose(out, in_, identb[:, :]),
                        after=after, inc=sT)

        def vop(f, after=()):
            return p.op("vector", f, after=after, inc=sV)

        def sop(f, after=()):
            return p.op("scalar", f, after=after, inc=sS)

        def gop(f, after=()):
            return p.op("gpsimd", f, after=after, inc=sG)

        def barrier():
            vals = [(c, c.n) for c in (sT, sV, sS, sG)]
            vals += dall("sync") + dall("gpsimd")
            for e in ENGINES:
                p.wait(e, vals)

        x_r = d["x"].rearrange("(i p) dd -> p i dd", p=128)
        R_r = R[:, :].rearrange("p (i dd) -> p i dd", dd=D)
        w_x = dma("sync", R_r, x_r)
        for nm, dst in [("c_PT", PTb), ("c_DT", DTb),
                        ("c_lamq", lamqb), ("c_lamk", lamkb), ("c_lam128", lam128b),
                        ("c_sqrtZ", sqrtZb), ("c_ident", identb), ("bq_f", bq_fb),
                        ("bk_f", bk_fb), ("bin_f", bin_fb), ("bs_f", bs_fb),
                        ("convw", convwb), ("convb", convbb)]:
            dma("gpsimd", dst[:, :], d[nm][:, :])
        for f in [lambda e: e.memset(onesb[:, :], 1.0),
                  lambda e: e.memset(onesc[:, :], 1.0),
                  lambda e: e.memset(onescf[:, :], 1.0),
                  lambda e: e.memset(onesrf[:, :], 1.0),
                  lambda e: e.memset(cons[:, 0:1], 1e-5),
                  lambda e: e.memset(cons[:, 1:2], 1.0 / D),
                  lambda e: e.memset(cons[:, 2:3], 1.0 / HD),
                  lambda e: e.memset(cons[:, 3:4], 127.0),
                  lambda e: e.memset(cons[:, 4:5], 1.0 / 127.0),
                  lambda e: e.memset(Sf[:, :], 0.0),
                  lambda e: e.memset(Sb[:, :], 0.0)]:
            vop(f)
        vop(lambda e: e.tensor_scalar_mul(nbin_fb[:, :], bin_fb[:, :], -1.0),
            after=dall("gpsimd"))
        w_cons = sV.n
        EPS = cons[:, 0:1]
        wait_init = [w_x, (sV, w_cons)] + dall("gpsimd")

        prep_state = {}

        def prep_pieces():
            w1r = d["w1T"].rearrange("(k p) m -> p k m", p=128)
            w2r = d["w2T"].rearrange("(k p) m -> p k m", p=128)
            pieces = []
            for k in range(8):
                for q4 in range(4):
                    pieces.append(("w1", w1r[:, k, 1024 * q4:1024 * (q4 + 1)],
                                   d["h_w1q"][:, k * 4096 + 1024 * q4:
                                              k * 4096 + 1024 * (q4 + 1)], k * 4 + q4))
            for i in range(32):
                pieces.append(("w2", w2r[:, i, :],
                               d["h_w2q"][:, 1024 * i:1024 * (i + 1)], i))
            return pieces

        def prep_p1_init():
            prep_state.update(pieces=prep_pieces(), red=[0, 0, 0], pos=0,
                              lanes=[ARF[:, 0:1024], ARF[:, 1024:2048], scs[:, :]])

        def prep_p1_step(n):
            for _ in range(n):
                pi = prep_state["pos"]
                if pi >= len(prep_state["pieces"]):
                    return
                wn, srcap, dstap, idx = prep_state["pieces"][pi]
                hb = pi % 3
                buf = prep_state["lanes"][hb]
                wld = dma("gpsimd", buf, srcap,
                          after=[(sV, prep_state["red"][hb])])
                col = idx if wn == "w1" else 32 + idx
                prep_state["red"][hb] = vop(
                    lambda e, col=col, buf=buf: e.tensor_reduce(
                        stat[:, col:col + 1], buf, axis=AX.X, op=ALU.add,
                        apply_absolute_value=True), after=[wld])
                prep_state["pos"] = pi + 1

        def emit_ln(i, dst_bf, after_r=(), stcol=32, st=None):
            Rt = R[:, i * D:(i + 1) * D]
            st = stat if st is None else st
            w1 = vop(lambda e: e.tensor_reduce(st[:, stcol:stcol + 1], Rt, axis=AX.X,
                                               op=ALU.add), after=after_r)
            w2 = vop(lambda e: e.scalar_tensor_tensor(AR3[:, 11264:12288], Rt, 1.0, Rt,
                                                      ALU.bypass, ALU.mult,
                                                      accum_out=st[:, stcol + 1:stcol + 2]),
                     after=after_r)
            w3 = vop(lambda e: e.tensor_single_scalar(st[:, stcol:stcol + 1],
                                                      st[:, stcol:stcol + 1],
                                                      1.0 / D, ALU.mult), after=[(sV, w1)])
            w4 = vop(lambda e: e.tensor_tensor(sbB[:, 0:1], st[:, stcol:stcol + 1],
                                               st[:, stcol:stcol + 1], ALU.mult),
                     after=[(sV, w3)])
            w5 = vop(lambda e: e.scalar_tensor_tensor(sbB[:, 1:2],
                                                      st[:, stcol + 1:stcol + 2],
                                                      1.0 / D, sbB[:, 0:1], ALU.mult,
                                                      ALU.subtract),
                     after=[(sV, w2), (sV, w4)])
            w6 = sop(lambda e: e.activation(sbB[:, 2:3], sbB[:, 1:2], AF.Sqrt,
                                            bias=EPS, scale=1.0), after=[(sV, w5)])
            w7 = vop(lambda e: e.reciprocal(sbB[:, 3:4], sbB[:, 2:3]), after=[(sS, w6)])
            wA = vop(lambda e: e.tensor_copy(st[:, stcol + 1:stcol + 2], sbB[:, 3:4]),
                     after=[(sV, w7)])
            wB = vop(lambda e: e.scalar_tensor_tensor(st[:, stcol:stcol + 1],
                                                      st[:, stcol:stcol + 1], -1.0,
                                                      sbB[:, 3:4], ALU.mult, ALU.mult),
                     after=[(sV, w7)])
            wapply = sop(lambda e: e.activation(dst_bf, Rt, AF.Identity,
                                                bias=st[:, stcol:stcol + 1],
                                                scale=st[:, stcol + 1:stcol + 2]),
                         after=[(sV, wA), (sV, wB)])
            return wapply

        bankuse = {}

        def guard(bank):
            return [bankuse.get(id(bank), (sV, 0))]

        def setguard(bank, ctr, val):
            bankuse[id(bank)] = (ctr, val)

        def emit_transpose_tile(i_local, src_bf, dst_fm, ncol, after_src):
            lastv = 0
            for j in range(8):
                bank = pt_b[j % 2]
                wt = trans(bank[:, 0:128], src_bf[:, 128 * j:128 * (j + 1)],
                           after=list(after_src) + guard(bank))
                wv = vop(lambda e, j=j, bank=bank: e.tensor_copy(
                    dst_fm[:, j * ncol + 128 * i_local: j * ncol + 128 * (i_local + 1)],
                    bank[:, 0:128]), after=[(sT, wt)])
                setguard(bank, sV, wv)
                lastv = wv
            return lastv

        def emit_attn_a(stop=None):
            STRIP = 512
            NSTR = S // STRIP
            xfm = AR1[:, 0:8 * 512]
            qtmp = AR1[:, 4096:8192]
            qrb = AR1[:, 8192:12288]
            krb = AR1[:, 12288:16384]
            rsc = AR3[:, 0:512]
            wq_r = d["wq"].rearrange("(k p) m -> p k m", p=128)
            wk_r = d["wk"].rearrange("(k p) m -> p k m", p=128)
            w_wq = dma("gpsimd", wbuf[:, 0:8 * D].rearrange("p (k m) -> p k m", m=D), wq_r)
            w_wk = dma("gpsimd", wbuf[:, 8 * D:16 * D].rearrange("p (k m) -> p k m", m=D),
                       wk_r)
            for si in range(NSTR):
                t0 = si * STRIP
                w_xfm = 0
                for il in range(4):
                    i = si * 4 + il
                    sg = stg[:, (il % 2) * D:((il % 2) + 1) * D]
                    wa = emit_ln(i, sg, after_r=wait_init, stcol=32 + (il % 2) * 2)
                    w_xfm = emit_transpose_tile(il, sg, xfm, 512, [(sS, wa)])
                xfm_h = d["h_xfm"].rearrange("p (k t) -> p k t", t=S)
                wsp = dma("sync", xfm_h[:, :, t0:t0 + STRIP],
                          xfm[:, :].rearrange("p (k t) -> p k t", t=512),
                          after=[(sV, w_xfm)])
                wsc1 = dma("sync", scs[:, 0:512], d["c_cos"][:, t0:t0 + 512],
                           after=[(sV, sV.n)])
                wsc2 = dma("sync", scs[:, 512:1024], d["c_sin"][:, t0:t0 + 512],
                           after=[])
                for woff, wld, bias_f, dsts, hdst in (
                        (0, w_wq, bq_fb, qrb, "h_qr"), (8 * D, w_wk, bk_fb, krb, "h_kr")):
                    for j in range(8):
                        bank = pb[j % 2]
                        bank2 = pb[2 + (j % 2)]
                        ga = guard(bank)
                        for k in range(8):
                            aft = ([wld, (sV, w_xfm)] + ga) if k == 0 else []
                            mm(bank[:, :],
                               wbuf[:, woff + k * D + 128 * j: woff + k * D + 128 * (j + 1)],
                               xfm[:, k * 512:(k + 1) * 512], start=(k == 0),
                               stop=(k == 7), after=aft)
                        wmmq = sT.n
                        we = sop(lambda e, j=j, bank=bank, bias_f=bias_f: e.activation(
                            qtmp[:, j * 512:(j + 1) * 512], bank[:, :], AF.Identity,
                            bias=bias_f[:, j:j + 1], scale=1.0), after=[(sT, wmmq)])
                        wrot = mm(bank2[:, :], PTb[:, :], qtmp[:, j * 512:(j + 1) * 512],
                                  start=True, stop=True, after=[(sS, we)] + guard(bank2))
                        wc = vop(lambda e, bank=bank, bias_f=bias_f, j=j: e.scalar_tensor_tensor(
                            ARF[:, 0:512], bank[:, :], bias_f[:, j:j + 1],
                            scs[:, 0:512], ALU.add, ALU.mult),
                            after=[(sT, wmmq), wsc1, (sS, we)])
                        setguard(bank, sV, wc)
                        wr = vop(lambda e, bank2=bank2: e.tensor_tensor(
                            ARF[:, 512:1024], bank2[:, :], scs[:, 512:1024], ALU.mult),
                            after=[(sT, wrot), wsc2])
                        setguard(bank2, sV, wr)
                        wadd = vop(lambda e, j=j, dsts=dsts: e.tensor_tensor(
                            dsts[:, j * 512:(j + 1) * 512], ARF[:, 0:512],
                            ARF[:, 512:1024], ALU.add), after=[(sV, wr), (sV, wc)])
                    qr_h = d[hdst].rearrange("p (k t) -> p k t", t=S)
                    dma("sync", qr_h[:, :, t0:t0 + STRIP],
                        dsts[:, :].rearrange("p (k t) -> p k t", t=512),
                        after=[(sV, sV.n)])
                barrier()
            if stop in ("qk", "qkcos", "qkrot", "qkstt"):
                return
            for wname, woffD, brow, hdst, do_silu in (
                    ("wv", 2 * D, "bv_r", "h_v", False), ("wg", 2 * D, "bg_r", "h_g", True)):
                w_r = d[wname].rearrange("(k p) m -> p k m", p=128)
                wld = dma("gpsimd", wbuf[:, :].rearrange("p (k m) -> p k m", m=2 * D), w_r,
                          after=[(sT, sT.n)])
                wbr = dma("gpsimd", rowb[:, :], d[brow][:, :])
                vg_spl = [None, None]
                barrier()
                for si in range(NSTR):
                    t0 = si * STRIP
                    xfm_h = d["h_xfm"].rearrange("p (k t) -> p k t", t=S)
                    wxb = dma("sync", xfm[:, :].rearrange("p (k t) -> p k t", t=512),
                              xfm_h[:, :, t0:t0 + STRIP], after=[(sV, sV.n), (sT, sT.n)])
                    for tc in range(4):
                        tok0 = 128 * tc
                        for n in range(4):
                            bank = pb[(tc * 4 + n) % 4]
                            ga = guard(bank)
                            for k in range(8):
                                aft = ([wld, wbr, wxb] + ga) if k == 0 else []
                                mm(bank[:, :],
                                   xfm[:, k * 512 + tok0:k * 512 + tok0 + 128],
                                   wbuf[:, k * 2 * D + 512 * n:k * 2 * D + 512 * (n + 1)],
                                   start=(k == 0), stop=False, after=aft)
                            mm(bank[:, :], onesb[:, :], rowb[:, 512 * n:512 * (n + 1)],
                               start=False, stop=True, after=[])
                            fsilu = AF.Silu if do_silu else AF.Identity
                            vb = (si * 4 + tc) % 2
                            we = sop(lambda e, bank=bank, n=n, fsilu=fsilu, vb=vb: e.activation(
                                AR3[:, 2048 * vb + 512 * n:2048 * vb + 512 * (n + 1)],
                                bank[:, :], fsilu), after=[(sT, sT.n), vg_spl[vb]])
                            setguard(bank, sS, we)
                        h_t = d[hdst].rearrange("p (i m) -> p i m", m=2 * D)
                        vb = (si * 4 + tc) % 2
                        vg_spl[vb] = dma("sync", h_t[:, si * 4 + tc, :],
                                         AR3[:, 2048 * vb:2048 * vb + 2048],
                                         after=[(sS, sS.n)])
            if stop == "vg":
                return
            wo_r = d["wo"].rearrange("(k p) m -> p k m", p=128)
            wldo = dma("gpsimd", wbuf[:, 0:16 * D].rearrange("p (k m) -> p k m", m=D),
                       wo_r, after=[(sT, sT.n)])
            wbr2 = dma("gpsimd", rowb2[:, :], d["bo_r"][:, :])
            cb_mix = AR3[:, 4096:6144]
            cb_mixf = AR3[:, 6144:8192]
            cb_kt = AR3[:, 8192:9216]
            cb_qt = AR3[:, 9216:10240]
            sco = AR3[:, 0:2048]
            scguard = {}
            qr_h = d["h_qr"].rearrange("p (k t) -> p k t", t=S)
            kr_h = d["h_kr"].rearrange("p (k t) -> p k t", t=S)
            v_h = d["h_v"].rearrange("p (i m) -> p i m", m=2 * D)
            g_h = d["h_g"].rearrange("p (i m) -> p i m", m=2 * D)
            bufg = {}
            seed = [(sT, sT.n), (sV, sV.n)]
            wsb_prev = 0
            for c in range(NC):
                tok0 = c * CH
                sset = c % 2
                b0 = 6144 * sset
                qrc = AR1[:, b0:b0 + 1024]
                krc = AR1[:, b0 + 1024:b0 + 2048]
                cb_v = AR1[:, b0 + 2048:b0 + 4096]
                cb_g = AR1[:, b0 + 4096:b0 + 6144]
                wq_c = dma("sync", qrc[:, :].rearrange("p (k t) -> p k t", t=128),
                           qr_h[:, :, tok0:tok0 + 128],
                           after=bufg.get(("q", sset), seed))
                wk_c = dma("sync", krc[:, :].rearrange("p (k t) -> p k t", t=128),
                           kr_h[:, :, tok0:tok0 + 128],
                           after=bufg.get(("k", sset), seed))
                wv_c = dma("sync", cb_v, v_h[:, c, :],
                           after=bufg.get(("v", sset), seed))
                wg_c = dma("sync", cb_g, g_h[:, c, :],
                           after=bufg.get(("g", sset), seed))
                if "pieces" in prep_state:
                    prep_p1_step(4)
                wkt = 0
                for j in range(8):
                    bank = pt_b[j % 2]
                    wt = trans(bank[:, 0:128], krc[:, 128 * j:128 * (j + 1)],
                               after=[wk_c] + guard(bank))
                    wkt = vop(lambda e, j=j, bank=bank: e.tensor_tensor(
                        cb_kt[:, 128 * j:128 * (j + 1)], bank[:, 0:128],
                        lamkb[:, 128 * j:128 * (j + 1)], ALU.mult), after=[(sT, wt)])
                    setguard(bank, sV, wkt)
                wqt = 0
                for hp in range(8):
                    wqt = vop(lambda e, hp=hp, qrc=qrc: e.tensor_tensor(
                        cb_qt[:, hp * 128:(hp + 1) * 128],
                        qrc[:, hp * 128:(hp + 1) * 128],
                        lamqb[:, hp * 128:(hp + 1) * 128], ALU.mult),
                        after=[wq_c])
                if stop == "ret1":
                    barrier()
                    continue
                wsc_h = [0] * 16
                for h in range(16):
                    hp, half = h // 2, h % 2
                    sbk = pb[h // 4]
                    sl = sbk[:, 128 * (h % 4):128 * (h % 4) + 128]
                    sga = guard(sbk) if h % 4 == 0 else []
                    wsc_h[h] = mm(sl,
                                  krc[64 * half:64 * half + 64, hp * 128:(hp + 1) * 128],
                                  qrc[64 * half:64 * half + 64, hp * 128:(hp + 1) * 128],
                                  start=True, stop=True, after=[wq_c, wk_c] + sga)
                last_wsc = wsc_h[15]
                wmask_h = [0] * 16
                for h in range(16):
                    sbk = pb[h // 4]
                    sl = sbk[:, 128 * (h % 4):128 * (h % 4) + 128]
                    wmask_h[h] = vop(lambda e, h=h, sl=sl: e.tensor_tensor(
                        sco[:, h * 128:(h + 1) * 128], sl,
                        DTb[:, h * CH:(h + 1) * CH], ALU.mult),
                        after=[(sT, wsc_h[(h // 4) * 4 + 3])])
                setguard(pb[0], sV, wmask_h[3])
                setguard(pb[1], sV, wmask_h[7])
                if stop == "ret2s":
                    barrier()
                    continue
                wapply_all = []
                grp_meta = []
                wsq = 0
                for gi, grp in enumerate(GRPS):
                    bank = pb[gi]
                    ga = guard(bank)
                    first = True
                    for idx, h in enumerate(grp):
                        hp, half = h // 2, h % 2
                        col = idx * 130
                        aft1 = ([(sV, wmask_h[15]), wv_c] + ga) if first \
                            else [(sV, wmask_h[h]), wv_c]
                        mm(bank[:, col:col + 128], sco[:, h * 128:(h + 1) * 128],
                           cb_v[:, h * 128:(h + 1) * 128], start=first, stop=False,
                           after=aft1)
                        first = False
                        mm(bank[:, col + 128:col + 129], sco[:, h * 128:(h + 1) * 128],
                           onesc[:, :], start=False, stop=False, after=[])
                        mm(bank[:, col:col + 128],
                           cb_qt[64 * half:64 * half + 64, hp * 128:(hp + 1) * 128],
                           Sb[64 * half:64 * half + 64, hp * 130:hp * 130 + 128],
                           start=False, stop=False,
                           after=[(sV, wqt), (sV, wsb_prev)])
                        mm(bank[:, col + 128:col + 129],
                           cb_qt[64 * half:64 * half + 64, hp * 128:(hp + 1) * 128],
                           Sb[64 * half:64 * half + 64, hp * 130 + 128:hp * 130 + 129],
                           start=False, stop=(idx == len(grp) - 1), after=[])
                    wmm_grp = sT.n
                    gsz = len(grp)
                    view3 = bank[:, 0:gsz * 130].rearrange("p (a b) -> p a b", b=130)
                    h0 = grp[0]
                    wmu = vop(lambda e, view3=view3, h0=h0, gsz=gsz: e.tensor_reduce(
                        epi[:, h0:h0 + gsz], view3[:, :, 0:128], axis=AX.X, op=ALU.add),
                        after=[(sT, wmm_grp)])
                    wrs = vop(lambda e, view3=view3, h0=h0, gsz=gsz: e.tensor_copy(
                        epi[:, 16 + h0:16 + h0 + gsz],
                        view3[:, :, 128:129].rearrange("p a b -> p (a b)")),
                        after=[(sT, wmm_grp)])
                    for idx, h in enumerate(grp):
                        wsq = sop(lambda e, idx=idx, h=h, bank=bank: e.activation(
                            junk[:, :], bank[:, idx * 130:idx * 130 + 128], AF.Square,
                            accum_out=epi[:, 32 + h:32 + h + 1]),
                            after=[(sT, wmm_grp), (sV, wrs), (sV, wmu)])
                    grp_meta.append((bank, grp, wmu, wrs))
                if stop == "ret2n":
                    barrier()
                    continue
                wstat = sV.n
                e_mu = epi[:, 0:16]
                e_rs = epi[:, 16:32]
                e_sq = epi[:, 32:48]
                e_scale = epi[:, 48:64]
                e_A = epi[:, 64:80]
                e_B = epi[:, 80:96]
                e_t1 = epi[:, 96:112]
                e_t2 = epi[:, 112:128]
                c1 = sop(lambda e, e_t1=e_t1, e_rs=e_rs: e.activation(
                    e_t1, e_rs, AF.Abs), after=[(sV, wstat)])
                c2 = vop(lambda e, c=c, e_t1=e_t1: e.tensor_tensor(
                    e_t1, e_t1, sqrtZb[:, c * 16:c * 16 + 16],
                    ALU.max), after=[(sS, c1)])
                c3 = vop(lambda e, e_scale=e_scale, e_t1=e_t1: e.reciprocal(
                    e_scale, e_t1), after=[(sV, c2)])
                c4 = vop(lambda e, e_mu=e_mu: e.tensor_single_scalar(
                    e_mu, e_mu, 1.0 / HD, ALU.mult), after=[(sV, wstat)])
                c5 = vop(lambda e, e_t1=e_t1, e_mu=e_mu: e.tensor_tensor(
                    e_t1, e_mu, e_mu, ALU.mult), after=[(sV, c4), (sV, c3)])
                c6 = vop(lambda e, e_t2=e_t2, e_sq=e_sq, e_t1=e_t1: e.scalar_tensor_tensor(
                    e_t2, e_sq, 1.0 / HD, e_t1, ALU.mult, ALU.subtract),
                    after=[(sS, wsq), (sV, c5)])
                c7 = vop(lambda e, e_t1=e_t1, e_scale=e_scale: e.tensor_tensor(
                    e_t1, e_scale, e_scale, ALU.mult), after=[(sV, c6)])
                c8 = vop(lambda e, e_t2=e_t2, e_t1=e_t1: e.tensor_tensor(
                    e_t2, e_t2, e_t1, ALU.mult), after=[(sV, c7)])
                c9 = sop(lambda e, e_t2=e_t2: e.activation(
                    e_t2, e_t2, AF.Sqrt, bias=EPS, scale=1.0), after=[(sV, c8)])
                c10 = vop(lambda e, e_t1=e_t1, e_t2=e_t2: e.reciprocal(
                    e_t1, e_t2), after=[(sS, c9)])
                c11 = vop(lambda e, e_A=e_A, e_scale=e_scale, e_t1=e_t1: e.tensor_tensor(
                    e_A, e_scale, e_t1, ALU.mult), after=[(sV, c10)])
                c12 = vop(lambda e, e_B=e_B, e_mu=e_mu, e_A=e_A: e.scalar_tensor_tensor(
                    e_B, e_mu, -1.0, e_A, ALU.mult, ALU.mult), after=[(sV, c11)])
                if stop == "ret2k":
                    barrier()
                    continue
                wap_last = 0
                for bank, grp, _, _ in grp_meta:
                    wap = 0
                    for idx, h in enumerate(grp):
                        wap = sop(lambda e, bank=bank, idx=idx, h=h: e.activation(
                            cb_mix[:, h * 128:(h + 1) * 128],
                            bank[:, idx * 130:idx * 130 + 128], AF.Identity,
                            bias=epi[:, 80 + h:80 + h + 1],
                            scale=epi[:, 64 + h:64 + h + 1]),
                            after=[(sV, c12)])
                    setguard(bank, sS, wap)
                    wap_last = wap
                if stop in ("ret2", "ret2a", "ret2b", "ret2c", "ret2d", "ret2e", "ret2f"):
                    barrier()
                    continue
                wstt_all = 0
                for hp in range(8):
                    bankA, bankB = pb[2 + 2 * (hp % 2)], pb[3 + 2 * (hp % 2)]
                    gaA, gaB = guard(bankA), guard(bankB)
                    h0, h1 = 2 * hp, 2 * hp + 1
                    mm(bankA[:, 0:128], cb_kt[:, hp * 128:(hp + 1) * 128],
                       cb_v[:, h0 * 128:(h0 + 1) * 128], start=True, stop=False,
                       after=[(sV, wkt), wv_c] + gaA)
                    mm(bankA[:, 128:129], cb_kt[:, hp * 128:(hp + 1) * 128], onesc[:, :],
                       start=False, stop=True, after=[])
                    mm(bankB[:, 0:128], cb_kt[:, hp * 128:(hp + 1) * 128],
                       cb_v[:, h1 * 128:(h1 + 1) * 128], start=True, stop=False,
                       after=gaB)
                    mm(bankB[:, 128:129], cb_kt[:, hp * 128:(hp + 1) * 128], onesc[:, :],
                       start=False, stop=True, after=[])
                    wd = sT.n
                    wsA = vop(lambda e, hp=hp, bankA=bankA: e.scalar_tensor_tensor(
                        Sf[0:64, hp * 130:hp * 130 + 129],
                        Sf[0:64, hp * 130:hp * 130 + 129],
                        lam128b[0:64, hp:hp + 1], bankA[0:64, 0:129],
                        ALU.mult, ALU.add), after=[(sT, wd)])
                    wsB = vop(lambda e, hp=hp, bankB=bankB: e.scalar_tensor_tensor(
                        Sf[64:128, hp * 130:hp * 130 + 129],
                        Sf[64:128, hp * 130:hp * 130 + 129],
                        lam128b[64:128, hp:hp + 1], bankB[64:128, 0:129],
                        ALU.mult, ALU.add), after=[(sT, wd)])
                    setguard(bankA, sV, wsA)
                    setguard(bankB, sV, wsB)
                    wstt_all = wsB
                st_done = sT.n
                wsb = vop(lambda e: e.tensor_copy(Sb[:, :], Sf[:, :]),
                          after=[(sT, sT.n), (sV, wstt_all)])
                wsb_prev = wsb
                bufg[("q", sset)] = [(sT, last_wsc), (sV, wqt)]
                bufg[("k", sset)] = [(sT, last_wsc)]
                bufg[("v", sset)] = [(sT, st_done)]
                if dbg == 2:
                    dma("sync", d["h_st"].rearrange("p (cc m) -> p cc m", m=8 * 130)[:, c, :],
                        Sf[:, :], after=[(sV, wsb)])
                if stop == "ret3":
                    barrier()
                    continue
                wgm = vop(lambda e, cb_g=cb_g: e.tensor_tensor(
                    cb_mix, cb_mix, cb_g, ALU.mult), after=[(sS, wap_last), wg_c])
                bufg[("g", sset)] = [(sV, wgm)]
                if stop == "ret5":
                    barrier()
                    continue
                if dbg == 2:
                    dma("sync", d["h_mix"].rearrange("p (c m) -> p c m", m=2 * D)[:, c, :],
                        cb_mix, after=[(sV, wgm)])
                    dma("sync", d["h_epi"].rearrange("p (c m) -> p c m", m=128)[:, c, :],
                        epi[:, :], after=[(sV, sV.n), (sS, sS.n)])
                wtr = 0
                for j in range(16):
                    bank = pt_b[j % 2]
                    wt = trans(bank[:, 0:128], cb_mix[:, 128 * j:128 * (j + 1)],
                               after=[(sV, wgm)] + guard(bank))
                    wtr = vop(lambda e, j=j, bank=bank: e.tensor_copy(
                        cb_mixf[:, 128 * j:128 * (j + 1)], bank[:, 0:128]),
                        after=[(sT, wt)])
                    setguard(bank, sV, wtr)
                for n in range(2):
                    bank = pb[4 + n]
                    ga = guard(bank)
                    for k in range(16):
                        aft = ([(sV, wtr), wldo, wbr2] + ga) if k == 0 else []
                        mm(bank[:, :], cb_mixf[:, 128 * k:128 * (k + 1)],
                           wbuf[:, k * D + 512 * n:k * D + 512 * (n + 1)],
                           start=(k == 0), stop=False, after=aft)
                    mm(bank[:, :], onesb[:, :], rowb2[:, 512 * n:512 * (n + 1)],
                       start=False, stop=True, after=[])
                    wr = vop(lambda e, bank=bank, c=c, n=n: e.tensor_tensor(
                        R[:, c * D + 512 * n:c * D + 512 * (n + 1)],
                        R[:, c * D + 512 * n:c * D + 512 * (n + 1)], bank[:, :],
                        ALU.add), after=[(sT, sT.n)])
                    setguard(bank, sV, wr)
            barrier()


        JUNK1024 = AR3[:, 11264:12288]

        def emit_ffn_prep():
            if "pieces" not in prep_state:
                prep_p1_init()
            prep_p1_step(len(prep_state["pieces"]))
            pieces = prep_state["pieces"]
            red = prep_state["red"]
            lanes = prep_state["lanes"]
            w_r1 = vop(lambda e: e.tensor_reduce(sbB[:, 40:41], stat[:, 0:32],
                                                 axis=AX.X, op=ALU.add),
                       after=[(sV, max(red))])
            w_r2 = vop(lambda e: e.tensor_reduce(sbB[:, 41:42], stat[:, 32:64],
                                                 axis=AX.X, op=ALU.add), after=[])
            ga = guard(pb[0])
            wm1 = mm(pb[0][0:1, 0:1], sbB[:, 40:41], onescf[:, :], start=True, stop=True,
                     after=[(sV, w_r1), (sV, w_r2)] + ga)
            wm2 = mm(pb[0][0:1, 1:2], sbB[:, 41:42], onescf[:, :], start=False, stop=True,
                     after=[])
            wt1 = vop(lambda e: e.tensor_scalar(sbB[0:1, 8:10], pb[0][0:1, 0:2],
                                                1.0 / (DH * D), 1e-5, ALU.mult, ALU.max),
                      after=[(sT, wm2)])
            setguard(pb[0], sV, wt1)
            wt2 = vop(lambda e: e.reciprocal(sbB[0:1, 10:12], sbB[0:1, 8:10]),
                      after=[(sV, wt1)])
            ga = guard(pb[1])
            wm3 = mm(pb[1][:, 0:4], onesrf[0:1, :], sbB[0:1, 8:12], start=True, stop=True,
                     after=[(sV, wt2)] + ga)
            wb4 = vop(lambda e: e.tensor_copy(wsc[:, 0:4], pb[1][:, 0:4]),
                      after=[(sT, wm3)])
            setguard(pb[1], sV, wb4)
            wcs = vop(lambda e: e.tensor_scalar(convwsc[:, :], convwb[:, :],
                                                wsc[:, 0:1], None, ALU.mult),
                      after=[(sV, wb4)] + dall("gpsimd"))
            lanes2 = lanes + [DTb[:, 0:1024], DTb[:, 1024:2048]]
            cvt = [wcs] * 5
            spl = [None] * 5
            for pi, (wn, srcap, dstap, idx) in enumerate(pieces):
                hb = pi % 5
                buf = lanes2[hb]
                qout = AR3[:, 1024 * hb:1024 * (hb + 1)]
                wld = dma("sync" if pi % 2 == 0 else "gpsimd", buf, srcap,
                          after=[(sV, cvt[hb])])
                swcol = 2 if wn == "w1" else 3
                wsp = sop(lambda e, swcol=swcol, buf=buf: e.activation(
                    buf, buf, AF.Identity, scale=wsc[:, swcol:swcol + 1]),
                    after=[wld, (sV, wcs)])
                wi = vop(lambda e, buf=buf: e.tensor_copy(i32s[:, :], buf),
                         after=[(sS, wsp)])
                cvt[hb] = wi
                wcl = vop(lambda e, qout=qout: e.tensor_scalar(
                    qout, i32s[:, :], -1.0, 1.0, ALU.max, ALU.min),
                    after=[(sV, wi), spl[hb]])
                spl[hb] = dma("sync" if pi % 2 == 0 else "gpsimd", dstap, qout,
                              after=[(sV, wcl)])
            barrier()

        def emit_ffn():
            xqfm = AR1[:, :]
            macc = AR3[:, 0:2048]
            mincc = AR3[:, 2048:4096]
            yring = ([AR3[:, 4096 + 256 * i:4096 + 256 * (i + 1)] for i in range(2)]
                     + [AR3[:, 6144 + 256 * i:6144 + 256 * (i + 1)] for i in range(2)])
            pring = ([AR3[:, 4608 + 272 * i:4608 + 272 * i + 258] for i in range(2)]
                     + [AR3[:, 6656 + 272 * i:6656 + 272 * i + 258] for i in range(2)])
            hsring = ([AR3[:, 5152 + 256 * i:5152 + 256 * (i + 1)] for i in range(2)]
                      + [AR3[:, 7200 + 256 * i:7200 + 256 * (i + 1)] for i in range(2)])
            hload = AR3[:, 6144:8192]
            hqb = AR3[:, 8192:10240]
            sbcols = AR3[:, 10496:10512]
            for i in range(NT):
                wa = emit_ln(i, ARF[:, 0:1024], after_r=wait_init, stcol=32 + (i % 2) * 2)
                wmx = vop(lambda e: e.tensor_reduce(sbB[:, 16:17], ARF[:, 0:1024],
                                                    axis=AX.X, op=ALU.max,
                                                    apply_absolute_value=True),
                          after=[(sS, wa)])
                wcl = vop(lambda e: e.tensor_scalar(sbB[:, 17:18], sbB[:, 16:17],
                                                    1e-5, 1.0 / 127.0, ALU.max, ALU.mult),
                          after=[(sV, wmx)])
                wrc = vop(lambda e: e.reciprocal(sbB[:, 18:19], sbB[:, 17:18]),
                          after=[(sV, wcl)])
                wr1 = vop(lambda e: e.tensor_scalar(ARF[:, 1024:2048], ARF[:, 0:1024],
                                                    sbB[:, 18:19], None, ALU.mult),
                          after=[(sV, wrc)])
                wi3 = vop(lambda e: e.tensor_copy(i32s[:, 0:1024], ARF[:, 1024:2048]),
                          after=[(sV, wr1)])
                sg = stg[:, (i % 2) * D:((i % 2) + 1) * D]
                wxq = vop(lambda e, sg=sg: e.tensor_scalar(
                    sg, i32s[:, 0:1024], sbB[:, 17:18], None, ALU.mult),
                    after=[(sV, wi3)])
                p.wait("tensor", [(sV, wxq)])
                emit_transpose_tile(i, sg, xqfm, 2048, [(sV, wxq)])
            barrier()
            winit1 = vop(lambda e: e.memset(macc[:, :], -3.0e38))
            winit2 = vop(lambda e: e.memset(mincc[:, :], 3.0e38))
            h_hs_v = d["h_hs"].rearrange("p (tc kk j) -> p tc kk j", kk=32, j=128)
            for half in range(2):
                wq1 = dma("gpsimd",
                          wbuf[:, :].rearrange("p (k m) -> p k m", m=2048),
                          d["h_w1q"].rearrange("p (k m) -> p k m", m=4096)[:, :, 2048 * half:2048 * (half + 1)],
                          after=[(sT, sT.n)])
                spl4 = [None] * 4
                for win in range(8):
                    a = win * 256 - 1
                    lo, hi = max(a, 0), min(a + 258, S)
                    off = lo - a
                    for c16 in range(16):
                        kk = half * 16 + c16
                        bank = pb[c16 % 4]
                        ga = guard(bank)
                        for k in range(8):
                            aft = ([wq1, (sV, winit2)] + ga) if k == 0 else []
                            mm(bank[:, off:off + hi - lo],
                               wbuf[:, k * 2048 + c16 * 128:k * 2048 + c16 * 128 + 128],
                               xqfm[:, k * 2048 + lo:k * 2048 + hi],
                               start=(k == 0), stop=(k == 7), after=aft)
                        wmm = sT.n
                        wfix = 0
                        if off == 1:
                            wfix = vop(lambda e, bank=bank: e.memset(bank[:, 0:1], 0.0),
                                       after=[(sT, wmm)])
                        elif hi - lo < 258:
                            wfix = vop(lambda e, bank=bank: e.memset(bank[:, 257:258], 0.0),
                                       after=[(sT, wmm)])
                        pr = pring[c16 % 4]
                        wev = sop(lambda e, bank=bank, pr=pr: e.activation(
                            pr, bank[:, 0:258], AF.Identity), after=[(sT, wmm), (sV, wfix)])
                        setguard(bank, sS, wev)
                        yr = yring[c16 % 4]
                        wt1 = vop(lambda e, pr=pr, yr=yr, kk=kk: e.tensor_scalar(
                            yr, pr[:, 1:257], convwsc[:, 3 * kk + 1:3 * kk + 2], None,
                            ALU.mult), after=[(sS, wev)])
                        wt2 = vop(lambda e, pr=pr, yr=yr, kk=kk: e.scalar_tensor_tensor(
                            yr, pr[:, 0:256], convwsc[:, 3 * kk:3 * kk + 1], yr,
                            ALU.mult, ALU.add), after=[(sV, wt1)])
                        wt3 = vop(lambda e, pr=pr, yr=yr, kk=kk: e.scalar_tensor_tensor(
                            yr, pr[:, 2:258], convwsc[:, 3 * kk + 2:3 * kk + 3], yr,
                            ALU.mult, ALU.add), after=[(sV, wt2)])
                        hs = hsring[c16 % 4]
                        wsl = sop(lambda e, yr=yr, hs=hs, kk=kk: e.activation(
                            hs, yr, AF.Silu, bias=convbb[:, kk:kk + 1], scale=1.0),
                            after=[(sV, wt3), spl4[c16 % 4]])
                        wmx = vop(lambda e, hs=hs, win=win: e.tensor_tensor(
                            macc[:, win * 256:(win + 1) * 256],
                            macc[:, win * 256:(win + 1) * 256], hs, ALU.max),
                            after=[(sS, wsl), (sV, winit1)])
                        wmn = vop(lambda e, hs=hs, win=win: e.tensor_tensor(
                            mincc[:, win * 256:(win + 1) * 256],
                            mincc[:, win * 256:(win + 1) * 256], hs, ALU.min),
                            after=[(sS, wsl), (sV, winit2)])
                        spl4[c16 % 4] = dma("sync",
                                            h_hs_v[:, 2 * win:2 * win + 2, kk, :],
                                            hs.rearrange("p (a j) -> p a j", j=128),
                                            after=[(sS, wsl), (sV, wmx), (sV, wmn)])
                barrier()
            for i in range(NT):
                bk = pt_b[i % 2]
                wtr = trans(bk[:, 0:128], macc[:, i * 128:(i + 1) * 128],
                            after=[(sV, sV.n), (sG, sG.n)] + guard(bk))
                wtr2 = trans(bk[:, 128:256], mincc[:, i * 128:(i + 1) * 128], after=[])
                wmx = vop(lambda e, bk=bk: e.tensor_reduce(
                    sbB[:, 20:21], bk[:, 0:128], axis=AX.X, op=ALU.max), after=[(sT, wtr2)])
                wmn = vop(lambda e, bk=bk: e.tensor_reduce(
                    sbB[:, 21:22], bk[:, 128:256], axis=AX.X, op=ALU.min), after=[(sT, wtr2)])
                setguard(bk, sV, max(wmx, wmn))
                wng = vop(lambda e: e.tensor_scalar(sbB[:, 21:22], sbB[:, 21:22], -1.0,
                                                    None, ALU.mult), after=[(sV, wmn)])
                wmm2 = vop(lambda e: e.tensor_tensor(sbB[:, 20:21], sbB[:, 20:21],
                                                     sbB[:, 21:22], ALU.max),
                           after=[(sV, wng), (sV, wmx)])
                wcl = vop(lambda e: e.tensor_scalar(sbB[:, 20:21], sbB[:, 20:21], 1e-5,
                                                    1.0 / 127.0, ALU.max, ALU.mult),
                          after=[(sV, wmm2)])
                wrc = vop(lambda e: e.reciprocal(sbB[:, 22:23], sbB[:, 20:21]),
                          after=[(sV, wcl)])
                wsb_ = vop(lambda e, i=i: e.tensor_copy(sbcols[:, i:i + 1], sbB[:, 22:23]),
                           after=[(sV, wrc)])
                winv = vop(lambda e, i=i: e.reciprocal(ffs[:, i:i + 1], sbcols[:, i:i + 1]),
                           after=[(sV, wsb_)])
                bk2 = pt_b[(i + 1) % 2]
                wtr = trans(bk2[0:1, 0:128], sbcols[:, i:i + 1],
                            after=[(sV, wsb_)] + guard(bk2))
                wsr = vop(lambda e, bk2=bk2, i=i: e.tensor_copy(
                    srowb[0:1, i * 128:(i + 1) * 128], bk2[0:1, 0:128]),
                    after=[(sT, wtr)])
                setguard(bk2, sV, wsr)
            if dbg == 2:
                dma("sync", d["h_ab"][:, 0:16], ffs[:, 0:16], after=[(sV, sV.n)])
            barrier()
            for half in range(2):
                wq2 = dma("gpsimd", wbuf[:, :],
                          d["h_w2q"][:, half * 16 * 1024:(half + 1) * 16 * 1024],
                          after=[(sT, sT.n)])
                prev_q = 0
                prev_r1 = 0
                for tc in range(NT):
                    whl = dma("sync", hload,
                              d["h_hs"][:, tc * 4096 + half * 2048:tc * 4096 + (half + 1) * 2048],
                              after=[(sV, prev_r1)])
                    gb = guard(pb[4])
                    wbc = mm(pb[4][:, 0:128], onesb[0:1, :],
                             srowb[0:1, tc * 128:(tc + 1) * 128],
                             start=True, stop=True, after=[(sV, sV.n)] + gb)
                    _h3 = hload.rearrange("p (a j) -> p a j", j=128)
                    _s3 = pb[4][:, 0:128].rearrange("p (a j) -> p a j", j=128)
                    _h3b, _s3b = broadcast_tensor_aps(_h3, _s3)
                    wr1 = vop(lambda e, _h3b=_h3b, _s3b=_s3b: e.tensor_tensor(
                        ARF[:, :].rearrange("p (a j) -> p a j", j=128),
                        _h3b, _s3b,
                        ALU.mult), after=[(sT, wbc), whl])
                    prev_r1 = wr1
                    setguard(pb[4], sV, wr1)
                    for hh in range(2):
                        wi3 = vop(lambda e, hh=hh: e.tensor_copy(
                            i32s[:, :], ARF[:, 1024 * hh:1024 * (hh + 1)]),
                            after=[(sV, wr1)])
                        prev_q = vop(lambda e, hh=hh: e.tensor_copy(
                            hqb[:, 1024 * hh:1024 * (hh + 1)], i32s[:, :]),
                            after=[(sV, wi3)])
                    if dbg == 2:
                        dma("sync", d["h_hq"][:, (half * NT + tc) * 2048:(half * NT + tc + 1) * 2048],
                            hqb, after=[(sV, prev_q)])
                    for n in range(2):
                        bank = pb[n]
                        ga = guard(bank)
                        for k16 in range(16):
                            aft = ([(sV, prev_q), wq2] + ga) if k16 == 0 else []
                            mm(bank[:, :], hqb[:, k16 * 128:(k16 + 1) * 128],
                               wbuf[:, k16 * 1024 + 512 * n:k16 * 1024 + 512 * (n + 1)],
                               start=(k16 == 0), stop=(k16 == 15), after=aft)
                        wsc_ = vop(lambda e, tc=tc: e.tensor_scalar(
                            sbB[:, 24:25], ffs[:, tc:tc + 1], wsc[:, 1:2], None,
                            ALU.mult), after=[(sT, sT.n)])
                        wra = vop(lambda e, bank=bank, tc=tc, n=n: e.scalar_tensor_tensor(
                            R[:, tc * D + 512 * n:tc * D + 512 * (n + 1)], bank[:, :],
                            sbB[:, 24:25], R[:, tc * D + 512 * n:tc * D + 512 * (n + 1)],
                            ALU.mult, ALU.add), after=[(sV, wsc_)])
                        setguard(bank, sV, wra)
                barrier()

        def emit_attn_b():
            STRIP = 512
            NSTR = S // STRIP
            xfmb = AR3[:, 0:4096]
            valfm = AR1[:, :]
            astrip = AR3[:, 4096:8192]
            svstrip = AR3[:, 8192:12288]
            carries = sbB[:, 32:40]
            aring = [ARF[:, 512 * i:512 * (i + 1)] for i in range(2)]
            winr = d["w_inT"].rearrange("(k p) m -> p k m", p=128)
            wldin = dma("gpsimd", wbuf[:, :].rearrange("p (k m) -> p k m", m=2 * D),
                        winr, after=[(sT, sT.n)])
            wcar = vop(lambda e: e.memset(carries[:, :], 0.0))
            h_ab_v = d["h_ab"].rearrange("p (j t) -> p j t", t=S)
            prev_asp = None
            for si in range(NSTR):
                t0 = si * STRIP
                w_xfm = 0
                for il in range(4):
                    i = si * 4 + il
                    sg = stg[:, (il % 2) * D:((il % 2) + 1) * D]
                    wa = emit_ln(i, sg, after_r=wait_init, stcol=2 * i, st=lnst)
                    w_xfm = emit_transpose_tile(il, sg, xfmb, 512, [(sS, wa)])
                for j in range(16):
                    bank = pb[j % 4]
                    ga = guard(bank)
                    for k in range(8):
                        aft = ([wldin, (sV, w_xfm)] + ga) if k == 0 else []
                        mm(bank[:, :], wbuf[:, k * 2 * D + 128 * j:k * 2 * D + 128 * (j + 1)],
                           xfmb[:, k * 512:(k + 1) * 512], start=(k == 0), stop=(k == 7),
                           after=aft)
                    wmm = sT.n
                    if j < 8:
                        ar = aring[j % 2]
                        wev = sop(lambda e, bank=bank, j=j, ar=ar: e.activation(
                            ar, bank[:, :], AF.Sigmoid, bias=nbin_fb[:, j:j + 1],
                            scale=-1.0), after=[(sT, wmm), prev_asp])
                        setguard(bank, sS, wev)
                        prev_asp = dma("sync", h_ab_v[:, j, t0:t0 + STRIP], ar,
                                       after=[(sS, wev)])
                    else:
                        jj = j - 8
                        wev = sop(lambda e, bank=bank, j=j, jj=jj, t0=t0: e.activation(
                            valfm[:, jj * 2048 + t0:jj * 2048 + t0 + STRIP], bank[:, :],
                            AF.Identity, bias=bin_fb[:, j:j + 1], scale=1.0),
                            after=[(sT, wmm)])
                        setguard(bank, sS, wev)
            barrier()
            wsr_ = d["w_sT"].rearrange("(k p) m -> p k m", p=128)
            wor_ = d["w_oT"].rearrange("(k p) m -> p k m", p=128)
            wlds = dma("gpsimd", wbuf[:, 0:8 * D].rearrange("p (k m) -> p k m", m=D),
                       wsr_, after=[(sT, sT.n)])
            wldo = dma("gpsimd", wbuf[:, 8 * D:16 * D].rearrange("p (k m) -> p k m", m=D),
                       wor_, after=[(sT, sT.n)])
            wbr2 = dma("gpsimd", rowb2[:, :], d["bo2_r"][:, :])
            prev_wr = [0]
            for si in range(NSTR):
                t0 = si * STRIP
                wla = dma("gpsimd", astrip[:, :].rearrange("p (j t) -> p j t", t=512),
                          h_ab_v[:, :, t0:t0 + STRIP], after=[(sV, sV.n)])
                for j in range(8):
                    bank = pb[4 + j % 2]
                    ga = guard(bank)
                    for k in range(8):
                        aft = ([wlds, (sT, sT.n)] + ga) if k == 0 else []
                        mm(bank[:, :], wbuf[:, k * D + 128 * j:k * D + 128 * (j + 1)],
                           valfm[:, k * 2048 + t0:k * 2048 + t0 + STRIP],
                           start=(k == 0), stop=(k == 7), after=aft)
                    wev = sop(lambda e, bank=bank, j=j: e.activation(
                        svstrip[:, j * 512:(j + 1) * 512], bank[:, :], AF.Tanh,
                        bias=bs_fb[:, j:j + 1], scale=1.0), after=[(sT, sT.n)])
                    setguard(bank, sS, wev)
                wsv = sS.n
                wsc_l = 0
                for j in range(8):
                    sl = slice(j * 512, (j + 1) * 512)
                    wt = vop(lambda e, sl=sl: e.tensor_tensor(
                        stg[:, 0:512], astrip[:, sl], svstrip[:, sl], ALU.mult),
                        after=[(sS, wsv), wla])
                    wb_ = vop(lambda e, sl=sl: e.tensor_tensor(
                        svstrip[:, sl], svstrip[:, sl], stg[:, 0:512], ALU.subtract),
                        after=[(sV, wt)])
                    wscan = vop(lambda e, sl=sl, j=j, si=si: e.tensor_tensor_scan(
                        svstrip[:, sl], astrip[:, sl], svstrip[:, sl],
                        0.0 if si == 0 else carries[:, j:j + 1],
                        ALU.mult, ALU.add), after=[(sV, wb_), (sV, wcar)])
                    wsc_l = vop(lambda e, sl=sl, j=j: e.tensor_copy(
                        carries[:, j:j + 1], svstrip[:, j * 512 + 511:j * 512 + 512]),
                        after=[(sV, wscan)])
                for tc in range(4):
                    i = si * 4 + tc
                    wx3 = sop(lambda e, i=i: e.activation(
                        ARF[:, 1024:2048], R[:, i * D:(i + 1) * D], AF.Identity,
                        bias=lnst[:, 2 * i:2 * i + 1], scale=lnst[:, 2 * i + 1:2 * i + 2]),
                        after=[(sV, wsc_l), (sV, prev_wr[0])])
                    for n in range(2):
                        bank = pb[n]
                        ga = guard(bank)
                        for k in range(8):
                            aft = ([wldo, wbr2, (sV, wsc_l)] + ga) if k == 0 else []
                            mm(bank[:, :],
                               svstrip[:, k * 512 + tc * 128:k * 512 + tc * 128 + 128],
                               wbuf[:, 8 * D + k * D + 512 * n:8 * D + k * D + 512 * (n + 1)],
                               start=(k == 0), stop=False, after=aft)
                        mm(bank[:, :], onesb[:, :], rowb2[:, 512 * n:512 * (n + 1)],
                           start=False, stop=True, after=[])
                        wadd = vop(lambda e, bank=bank, i=i, n=n: e.tensor_tensor(
                            ARF[:, 1024 + 512 * n:1024 + 512 * (n + 1)],
                            ARF[:, 1024 + 512 * n:1024 + 512 * (n + 1)], bank[:, :],
                            ALU.add), after=[(sT, sT.n), (sS, wx3)])
                        setguard(bank, sV, wadd)
                        wr_ = vop(lambda e, i=i, n=n: e.tensor_tensor(
                            R[:, i * D + 512 * n:i * D + 512 * (n + 1)],
                            R[:, i * D + 512 * n:i * D + 512 * (n + 1)],
                            ARF[:, 1024 + 512 * n:1024 + 512 * (n + 1)], ALU.add),
                            after=[(sV, wadd)])
                        prev_wr[0] = wr_
            barrier()

        ffn_prepped = [False]
        if any(p.startswith("ffn") for p in parts):
            prep_p1_init()
        for pi_, prt in enumerate(parts):
            if prt.startswith("attn_a"):
                emit_attn_a(stop=prt.split(":")[1] if ":" in prt else None)
            elif prt.startswith("ffn"):
                if not ffn_prepped[0]:
                    emit_ffn_prep()
                    ffn_prepped[0] = True
                emit_ffn()
            elif prt.startswith("attn_b"):
                emit_attn_b()
            if dbg == 2 and pi_ < 4:
                dma("sync", d[f"h_R{pi_}"][:, :], R[:, :],
                    after=[(sV, sV.n), (sS, sS.n), (sT, sT.n), (sG, sG.n)])
                barrier()

        out_r = d["out"].rearrange("(i p) dd -> p i dd", p=128)
        wst = dma("sync", out_r, R_r, after=[(sV, sV.n), (sS, sS.n), (sT, sT.n),
                                             (sG, sG.n)])
        p.wait("sync", [wst])

        for eng_name in ENGINES:
            fns = list(p.streams[eng_name])

            def body(eng, fns=fns):
                for f in fns:
                    f(eng)
            getattr(blk, eng_name)(body)

    return nc, d


def host_inputs(inputs):
    g = dict(inputs)
    out = {}
    out["wq"] = np.ascontiguousarray(g["L0_attn_a_Wq"], np.float32)
    out["wk"] = np.ascontiguousarray(g["L0_attn_a_Wk"], np.float32)
    out["wv"] = np.ascontiguousarray(g["L0_attn_a_Wv"], np.float32)
    out["wg"] = np.ascontiguousarray(g["L0_attn_a_Wg"], np.float32)
    out["wo"] = np.ascontiguousarray(g["L0_attn_a_Wo"], np.float32)
    out["bq_f"] = np.ascontiguousarray(g["L0_attn_a_bq"].reshape(8, 128).T, np.float32)
    out["bk_f"] = np.ascontiguousarray(g["L0_attn_a_bk"].reshape(8, 128).T, np.float32)
    out["bv_r"] = np.ascontiguousarray(g["L0_attn_a_bv"].reshape(1, 2 * D), np.float32)
    out["bg_r"] = np.ascontiguousarray(g["L0_attn_a_bg"].reshape(1, 2 * D), np.float32)
    out["bo_r"] = np.ascontiguousarray(g["L0_attn_a_bo"].reshape(1, D), np.float32)
    out["w_inT"] = np.ascontiguousarray(np.asarray(g["L0_attn_b_W_in"]).T, np.float32)
    out["bin_f"] = np.ascontiguousarray(g["L0_attn_b_b_in"].reshape(16, 128).T, np.float32)
    out["w_sT"] = np.ascontiguousarray(np.asarray(g["L0_attn_b_W_s"]).T, np.float32)
    out["bs_f"] = np.ascontiguousarray(g["L0_attn_b_b_s"].reshape(8, 128).T, np.float32)
    out["w_oT"] = np.ascontiguousarray(np.asarray(g["L0_attn_b_W_o"]).T, np.float32)
    out["bo2_r"] = np.ascontiguousarray(g["L0_attn_b_b_o"].reshape(1, D), np.float32)
    out["w1T"] = np.ascontiguousarray(np.asarray(g["L0_ffn_w1"]).T, np.float32)
    out["w2T"] = np.ascontiguousarray(np.asarray(g["L0_ffn_w2"]).T, np.float32)
    cw = np.asarray(g["L0_ffn_conv_w"]).reshape(DH, 3)
    cwt = cw.reshape(32, 128, 3).transpose(1, 0, 2).reshape(128, 96)
    out["convw"] = np.ascontiguousarray(cwt, np.float32)
    out["convb"] = np.ascontiguousarray(
        np.asarray(g["L0_ffn_conv_b"]).reshape(32, 128).T, np.float32)
    out.update(host_constants())
    return out


_CACHE = {}


def kernel(**inputs):
    if "prog" not in _CACHE:
        _CACHE["prog"] = build()
    nc, _ = _CACHE["prog"]
    shared = host_inputs(inputs)
    x = np.ascontiguousarray(np.asarray(inputs["L0_attn_a_x"]), np.float32)
    in_maps = []
    for c in range(8):
        m = dict(shared)
        m["x"] = np.ascontiguousarray(x[c])
        in_maps.append(m)
    res = run_bass_kernel_spmd(nc, in_maps, core_ids=list(range(8)))
    out = np.stack([np.asarray(r["out"]) for r in res.results], axis=0)
    return out.astype(np.float32)


# version 13 (final)
# speedup vs baseline: 1.3393x; 1.0016x over previous
import numpy as np
import concourse.bass as bass
import concourse.mybir as mybir
from concourse.bass import broadcast_tensor_aps
from concourse.bass_utils import run_bass_kernel_spmd

F32 = mybir.dt.float32
BF16 = mybir.dt.bfloat16
I32 = mybir.dt.int32
AX = mybir.AxisListType
AF = mybir.ActivationFunctionType
ALU = mybir.AluOpType

S, D, H, KD, HD, DH = 2048, 1024, 16, 64, 128, 4096
NT = S // 128
CH = 128
NC = S // CH
ENGINES = ("tensor", "vector", "scalar", "gpsimd", "sync")
GRPS = [(0, 1, 2), (3, 4, 5), (6, 7, 8), (9, 10, 11), (12, 13, 14), (15,)]


class Ctr:
    def __init__(self, handle):
        self.h = handle
        self.n = 0

    def tick(self, k=1):
        self.n += k
        return self.n


class Prog:
    def __init__(self, nc):
        self.nc = nc
        self.streams = {e: [] for e in ENGINES}

    def op(self, engine, f, after=(), inc=None, inc_k=1):
        after = [t for t in after if t is not None]
        after = [(c, v) for (c, v) in after if v > 0]
        for c, v in after:
            assert v <= c.n, f"wait on future sem value {v} > {c.n}"
        val = inc.tick(inc_k) if inc is not None else None
        semh = inc.h if inc is not None else None

        def emit(eng, after=after, f=f, semh=semh, inc_k=inc_k):
            for c, v in after:
                eng.wait_ge(c.h, v)
            ins = f(eng)
            if semh is not None:
                ins.then_inc(semh, inc_k)
        self.streams[engine].append(emit)
        return val

    def wait(self, engine, after):
        after = [t for t in after if t is not None]
        after = [(c, v) for (c, v) in after if v > 0]
        for c, v in after:
            assert v <= c.n

        def emit(eng, after=after):
            for c, v in after:
                eng.wait_ge(c.h, v)
        self.streams[engine].append(emit)


def host_constants():
    import ml_dtypes
    bf = ml_dtypes.bfloat16
    angle = (1.0 / 10000.0 ** np.linspace(0.0, 1.0, KD // 2)).astype(np.float64)
    angle = np.repeat(angle, 2)
    idx = np.arange(S, dtype=np.float64)
    sin = np.sin(idx[:, None] * angle[None, :])
    cos = np.cos(idx[:, None] * angle[None, :])
    sin_fm = np.ascontiguousarray(np.tile(sin.T, (2, 1))).astype(np.float32)
    cos_fm = np.ascontiguousarray(np.tile(cos.T, (2, 1))).astype(np.float32)
    PT = np.zeros((128, 128), np.float32)
    for m in range(128):
        if m % 2 == 0:
            PT[m + 1, m] = -1.0
        else:
            PT[m - 1, m] = 1.0
    PT = PT.astype(bf)
    decay = np.log(1.0 - 2.0 ** (-5.0 - np.arange(H, dtype=np.float64)))
    lam = np.exp(decay)
    t_i = np.arange(CH)
    diff = t_i[None, :] - t_i[:, None]
    DT = np.where(diff >= 0, np.exp(decay[:, None, None] * diff[None]), 0.0) * 0.125
    DT_sb = np.ascontiguousarray(DT.transpose(1, 0, 2).reshape(CH, H * CH)).astype(np.float32)
    tq = np.arange(CH, dtype=np.float64) + 1.0
    lamq = np.zeros((128, 8 * CH), np.float64)
    for hp in range(8):
        lamq[0:64, hp * CH:(hp + 1) * CH] = lam[2 * hp] ** tq[None, :]
        lamq[64:128, hp * CH:(hp + 1) * CH] = lam[2 * hp + 1] ** tq[None, :]
    lamq = lamq.astype(bf)
    lamk = np.zeros((128, D), np.float64)
    tk = np.arange(CH, dtype=np.float64)
    for h in range(H):
        lamk[:, h * KD:(h + 1) * KD] = (0.125 * lam[h] ** (127.0 - tk))[:, None]
    lamk = lamk.astype(bf)
    lam128 = np.zeros((128, 8), np.float64)
    for hp in range(8):
        lam128[0:64, hp] = lam[2 * hp] ** 128.0
        lam128[64:128, hp] = lam[2 * hp + 1] ** 128.0
    lam128 = lam128.astype(np.float32)
    s_i = np.arange(S, dtype=np.float64)
    Z = (1.0 - lam[None, :] ** (s_i[:, None] + 1.0)) / (1.0 - lam[None, :])
    sqrtZ_sb = np.ascontiguousarray(np.sqrt(Z).reshape(NC, 128, H).transpose(1, 0, 2)
                                    .reshape(128, NC * H)).astype(np.float32)
    ident = np.eye(128, dtype=bf)
    return dict(c_sin=np.asarray(sin_fm), c_cos=np.asarray(cos_fm), c_PT=np.asarray(PT),
                c_DT=np.asarray(DT_sb), c_lamq=np.asarray(lamq), c_lamk=np.asarray(lamk),
                c_lam128=lam128, c_sqrtZ=sqrtZ_sb, c_ident=np.asarray(ident))


def build(parts=("attn_a", "ffn1", "attn_b", "ffn2"), dbg=False):
    import contextlib
    nc = bass.Bass("TRN2", target_bir_lowering=False)
    d = {}

    def din(name, shape, dtype=F32):
        d[name] = nc.dram_tensor(name, list(shape), dtype, kind="ExternalInput").ap()
        return d[name]

    def dscr(name, shape, dtype):
        d[name] = nc.dram_tensor(name, list(shape), dtype,
                                 kind="ExternalOutput" if dbg else "Internal").ap()
        return d[name]

    din("x", (S, D))
    din("wq", (D, D)); din("wk", (D, D)); din("wv", (D, 2 * D)); din("wg", (D, 2 * D))
    din("wo", (2 * D, D))
    din("bq_f", (128, 8)); din("bk_f", (128, 8))
    din("bv_r", (1, 2 * D)); din("bg_r", (1, 2 * D)); din("bo_r", (1, D))
    din("w_inT", (D, 2 * D)); din("w_sT", (D, D)); din("w_oT", (D, D))
    din("bin_f", (128, 16)); din("bs_f", (128, 8)); din("bo2_r", (1, D))
    din("w1T", (D, DH)); din("w2T", (DH, D))
    din("convw", (128, 96)); din("convb", (128, 32))
    din("c_sin", (128, S)); din("c_cos", (128, S))
    din("c_PT", (128, 128), BF16); din("c_DT", (128, H * CH))
    din("c_lamq", (128, 8 * CH), BF16); din("c_lamk", (128, D), BF16)
    din("c_lam128", (128, 8)); din("c_sqrtZ", (128, NC * H))
    din("c_ident", (128, 128), BF16)
    d["out"] = nc.dram_tensor("out", [S, D], F32, kind="ExternalOutput").ap()
    dscr("h_xfm", (128, 8 * S), BF16)
    dscr("h_qr", (128, 8 * S), BF16)
    dscr("h_kr", (128, 8 * S), BF16)
    dscr("h_v", (128, NT * 2 * D), BF16)
    dscr("h_g", (128, NT * 2 * D), BF16)
    dscr("h_w1q", (128, 8 * DH), BF16)
    dscr("h_w2q", (128, 32 * D), BF16)
    dscr("h_hs", (128, 32 * S), BF16)
    dscr("h_ab", (128, 16 * S // 2), F32)
    if dbg:
        dscr("h_st", (128, NC * 8 * 130), F32)
        dscr("h_mix", (128, NC * 2 * D), BF16)
        dscr("h_epi", (128, NC * 128), F32)
        dscr("h_sco", (128, NC * 2 * CH), BF16)
        dscr("h_hq", (128, 32 * 2048), BF16)
        for _i in range(4):
            dscr(f"h_R{_i}", (128, NT * D), F32)

    stack = contextlib.ExitStack()
    with stack:
        def sbuf(name, shape, dtype):
            return stack.enter_context(nc.sbuf_tensor(name, list(shape), dtype))

        def psum_t(name, shape, dtype=F32):
            return stack.enter_context(nc.psum_tensor(name, list(shape), dtype))

        R = sbuf("R", (128, NT * D), F32)
        wbuf = sbuf("wbuf", (128, 8 * 2 * D), BF16)
        AR1 = sbuf("AR1", (128, 16384), BF16)
        AR3 = sbuf("AR3", (128, 12288), BF16)
        ARF = sbuf("ARF", (128, 2048), F32)
        stg = sbuf("stg", (128, 2 * D), BF16)
        Sf = sbuf("Sf", (128, 8 * 130), F32)
        Sb = sbuf("Sb", (128, 8 * 130), BF16)
        junk = sbuf("junk", (128, 128), BF16)
        stat = sbuf("stat", (128, 64), F32)
        epi = sbuf("epi", (128, 16 * 8), F32)
        cons = sbuf("cons", (128, 8), F32)
        onesb = sbuf("onesb", (1, 128), BF16)
        onesc = sbuf("onesc", (128, 1), BF16)
        onescf = sbuf("onescf", (128, 1), F32)
        onesrf = sbuf("onesrf", (1, 128), F32)
        srowb = sbuf("srowb", (1, 2048), BF16)
        scs = sbuf("scs", (128, 1024), F32)
        wsc = sbuf("wsc", (128, 8), F32)
        rowb = sbuf("rowb", (1, 2 * D), BF16)
        rowb2 = sbuf("rowb2", (1, D), BF16)
        PTb = sbuf("PTb", (128, 128), BF16)
        DTb = sbuf("DTb", (128, H * CH), F32)
        lamqb = sbuf("lamqb", (128, 8 * CH), BF16)
        lamkb = sbuf("lamkb", (128, D), BF16)
        lam128b = sbuf("lam128b", (128, 8), F32)
        sqrtZb = sbuf("sqrtZb", (128, NC * H), F32)
        identb = sbuf("identb", (128, 128), BF16)
        bq_fb = sbuf("bq_fb", (128, 8), F32)
        bk_fb = sbuf("bk_fb", (128, 8), F32)
        bin_fb = sbuf("bin_fb", (128, 16), F32)
        nbin_fb = sbuf("nbin_fb", (128, 16), F32)
        bs_fb = sbuf("bs_fb", (128, 8), F32)
        convwb = sbuf("convwb", (128, 96), F32)
        convwsc = sbuf("convwsc", (128, 96), F32)
        convbb = sbuf("convbb", (128, 32), F32)
        ffs = sbuf("ffs", (128, NT * 4), F32)
        lnst = sbuf("lnst", (128, NT * 2), F32)
        i32s = sbuf("i32s", (128, 1024), I32)
        sbB = sbuf("sbB", (128, 64), F32)
        pb = [psum_t(f"pb{i}", (128, 512), F32) for i in range(6)]
        pt_b = [psum_t("pt0", (128, 1024), BF16), psum_t("pt1", (128, 1024), BF16)]

        sems = {}
        for nm in ("t", "v", "s", "g"):
            sems[nm] = Ctr(stack.enter_context(nc.semaphore("sem_" + nm)))
        dpool = {"sync": [Ctr(stack.enter_context(nc.semaphore(f"sem_ds{i}")))
                          for i in range(8)],
                 "gpsimd": [Ctr(stack.enter_context(nc.semaphore(f"sem_dg{i}")))
                            for i in range(4)]}
        didx = {"sync": 0, "gpsimd": 0}
        blk = stack.enter_context(nc.Block())
        p = Prog(nc)
        sT, sV, sS, sG = (sems[k] for k in ("t", "v", "s", "g"))

        def dma(queue, dst, src, after=()):
            c = dpool[queue][didx[queue] % len(dpool[queue])]
            didx[queue] += 1
            v = p.op(queue, lambda e: e.dma_start(out=dst, in_=src), after=after,
                     inc=c, inc_k=16)
            return (c, v)

        def dall(queue):
            return [(c, c.n) for c in dpool[queue]]

        def mm(out, lhsT, rhs, start, stop, after=(), tp=None):
            if tp is None:
                return p.op("tensor", lambda e: e.matmul(out, lhsT, rhs, start=start,
                                                         stop=stop), after=after, inc=sT)
            return p.op("tensor", lambda e: e.matmul(out, lhsT, rhs, start=start,
                                                     stop=stop, tile_position=tp),
                        after=after, inc=sT)

        def trans(out, in_, after=()):
            return p.op("tensor", lambda e: e.transpose(out, in_, identb[:, :]),
                        after=after, inc=sT)

        def vop(f, after=()):
            return p.op("vector", f, after=after, inc=sV)

        def sop(f, after=()):
            return p.op("scalar", f, after=after, inc=sS)

        def gop(f, after=()):
            return p.op("gpsimd", f, after=after, inc=sG)

        def barrier():
            vals = [(c, c.n) for c in (sT, sV, sS, sG)]
            vals += dall("sync") + dall("gpsimd")
            for e in ENGINES:
                p.wait(e, vals)

        x_r = d["x"].rearrange("(i p) dd -> p i dd", p=128)
        R_r = R[:, :].rearrange("p (i dd) -> p i dd", dd=D)
        w_x = dma("sync", R_r, x_r)
        for nm, dst in [("c_PT", PTb), ("c_DT", DTb),
                        ("c_lamq", lamqb), ("c_lamk", lamkb), ("c_lam128", lam128b),
                        ("c_sqrtZ", sqrtZb), ("c_ident", identb), ("bq_f", bq_fb),
                        ("bk_f", bk_fb), ("bin_f", bin_fb), ("bs_f", bs_fb),
                        ("convw", convwb), ("convb", convbb)]:
            dma("gpsimd", dst[:, :], d[nm][:, :])
        for f in [lambda e: e.memset(onesb[:, :], 1.0),
                  lambda e: e.memset(onesc[:, :], 1.0),
                  lambda e: e.memset(onescf[:, :], 1.0),
                  lambda e: e.memset(onesrf[:, :], 1.0),
                  lambda e: e.memset(cons[:, 0:1], 1e-5),
                  lambda e: e.memset(cons[:, 1:2], 1.0 / D),
                  lambda e: e.memset(cons[:, 2:3], 1.0 / HD),
                  lambda e: e.memset(cons[:, 3:4], 127.0),
                  lambda e: e.memset(cons[:, 4:5], 1.0 / 127.0),
                  lambda e: e.memset(Sf[:, :], 0.0),
                  lambda e: e.memset(Sb[:, :], 0.0)]:
            vop(f)
        vop(lambda e: e.tensor_scalar_mul(nbin_fb[:, :], bin_fb[:, :], -1.0),
            after=dall("gpsimd"))
        w_cons = sV.n
        EPS = cons[:, 0:1]
        wait_init = [w_x, (sV, w_cons)] + dall("gpsimd")

        prep_state = {}

        def prep_pieces():
            w1r = d["w1T"].rearrange("(k p) m -> p k m", p=128)
            w2r = d["w2T"].rearrange("(k p) m -> p k m", p=128)
            pieces = []
            for k in range(8):
                for q4 in range(4):
                    pieces.append(("w1", w1r[:, k, 1024 * q4:1024 * (q4 + 1)],
                                   d["h_w1q"][:, k * 4096 + 1024 * q4:
                                              k * 4096 + 1024 * (q4 + 1)], k * 4 + q4))
            for i in range(32):
                pieces.append(("w2", w2r[:, i, :],
                               d["h_w2q"][:, 1024 * i:1024 * (i + 1)], i))
            return pieces

        def prep_p1_init():
            prep_state.update(pieces=prep_pieces(), red=[0, 0, 0], pos=0,
                              lanes=[ARF[:, 0:1024], ARF[:, 1024:2048], scs[:, :]])

        def prep_p1_step(n):
            for _ in range(n):
                pi = prep_state["pos"]
                if pi >= len(prep_state["pieces"]):
                    return
                wn, srcap, dstap, idx = prep_state["pieces"][pi]
                hb = pi % 3
                buf = prep_state["lanes"][hb]
                wld = dma("gpsimd", buf, srcap,
                          after=[(sV, prep_state["red"][hb])])
                col = idx if wn == "w1" else 32 + idx
                prep_state["red"][hb] = vop(
                    lambda e, col=col, buf=buf: e.tensor_reduce(
                        stat[:, col:col + 1], buf, axis=AX.X, op=ALU.add,
                        apply_absolute_value=True), after=[wld])
                prep_state["pos"] = pi + 1

        def emit_ln(i, dst_bf, after_r=(), stcol=32, st=None):
            Rt = R[:, i * D:(i + 1) * D]
            st = stat if st is None else st
            w1 = vop(lambda e: e.tensor_reduce(st[:, stcol:stcol + 1], Rt, axis=AX.X,
                                               op=ALU.add), after=after_r)
            w2 = vop(lambda e: e.scalar_tensor_tensor(AR3[:, 11264:12288], Rt, 1.0, Rt,
                                                      ALU.bypass, ALU.mult,
                                                      accum_out=st[:, stcol + 1:stcol + 2]),
                     after=after_r)
            w3 = vop(lambda e: e.tensor_single_scalar(st[:, stcol:stcol + 1],
                                                      st[:, stcol:stcol + 1],
                                                      1.0 / D, ALU.mult), after=[(sV, w1)])
            w4 = vop(lambda e: e.tensor_tensor(sbB[:, 0:1], st[:, stcol:stcol + 1],
                                               st[:, stcol:stcol + 1], ALU.mult),
                     after=[(sV, w3)])
            w5 = vop(lambda e: e.scalar_tensor_tensor(sbB[:, 1:2],
                                                      st[:, stcol + 1:stcol + 2],
                                                      1.0 / D, sbB[:, 0:1], ALU.mult,
                                                      ALU.subtract),
                     after=[(sV, w2), (sV, w4)])
            w6 = sop(lambda e: e.activation(sbB[:, 2:3], sbB[:, 1:2], AF.Sqrt,
                                            bias=EPS, scale=1.0), after=[(sV, w5)])
            w7 = vop(lambda e: e.reciprocal(sbB[:, 3:4], sbB[:, 2:3]), after=[(sS, w6)])
            wA = vop(lambda e: e.tensor_copy(st[:, stcol + 1:stcol + 2], sbB[:, 3:4]),
                     after=[(sV, w7)])
            wB = vop(lambda e: e.scalar_tensor_tensor(st[:, stcol:stcol + 1],
                                                      st[:, stcol:stcol + 1], -1.0,
                                                      sbB[:, 3:4], ALU.mult, ALU.mult),
                     after=[(sV, w7)])
            wapply = sop(lambda e: e.activation(dst_bf, Rt, AF.Identity,
                                                bias=st[:, stcol:stcol + 1],
                                                scale=st[:, stcol + 1:stcol + 2]),
                         after=[(sV, wA), (sV, wB)])
            return wapply

        bankuse = {}

        def guard(bank):
            return [bankuse.get(id(bank), (sV, 0))]

        def setguard(bank, ctr, val):
            bankuse[id(bank)] = (ctr, val)

        def emit_transpose_tile(i_local, src_bf, dst_fm, ncol, after_src):
            lastv = 0
            for j in range(8):
                bank = pt_b[j % 2]
                wt = trans(bank[:, 0:128], src_bf[:, 128 * j:128 * (j + 1)],
                           after=list(after_src) + guard(bank))
                wv = vop(lambda e, j=j, bank=bank: e.tensor_copy(
                    dst_fm[:, j * ncol + 128 * i_local: j * ncol + 128 * (i_local + 1)],
                    bank[:, 0:128]), after=[(sT, wt)])
                setguard(bank, sV, wv)
                lastv = wv
            return lastv

        def emit_attn_a(stop=None):
            STRIP = 512
            NSTR = S // STRIP
            xfm = AR1[:, 0:8 * 512]
            qtmp = AR1[:, 4096:8192]
            qrb = AR1[:, 8192:12288]
            krb = AR1[:, 12288:16384]
            rsc = AR3[:, 0:512]
            wq_r = d["wq"].rearrange("(k p) m -> p k m", p=128)
            wk_r = d["wk"].rearrange("(k p) m -> p k m", p=128)
            w_wq = dma("gpsimd", wbuf[:, 0:8 * D].rearrange("p (k m) -> p k m", m=D), wq_r)
            w_wk = dma("gpsimd", wbuf[:, 8 * D:16 * D].rearrange("p (k m) -> p k m", m=D),
                       wk_r)
            for si in range(NSTR):
                t0 = si * STRIP
                w_xfm = 0
                for il in range(4):
                    i = si * 4 + il
                    sg = stg[:, (il % 2) * D:((il % 2) + 1) * D]
                    wa = emit_ln(i, sg, after_r=wait_init, stcol=32 + (il % 2) * 2)
                    w_xfm = emit_transpose_tile(il, sg, xfm, 512, [(sS, wa)])
                xfm_h = d["h_xfm"].rearrange("p (k t) -> p k t", t=S)
                wsp = dma("sync", xfm_h[:, :, t0:t0 + STRIP],
                          xfm[:, :].rearrange("p (k t) -> p k t", t=512),
                          after=[(sV, w_xfm)])
                wsc1 = dma("sync", scs[:, 0:512], d["c_cos"][:, t0:t0 + 512],
                           after=[(sV, sV.n)])
                wsc2 = dma("sync", scs[:, 512:1024], d["c_sin"][:, t0:t0 + 512],
                           after=[])
                for woff, wld, bias_f, dsts, hdst in (
                        (0, w_wq, bq_fb, qrb, "h_qr"), (8 * D, w_wk, bk_fb, krb, "h_kr")):
                    for j in range(8):
                        bank = pb[j % 2]
                        bank2 = pb[2 + (j % 2)]
                        ga = guard(bank)
                        for k in range(8):
                            aft = ([wld, (sV, w_xfm)] + ga) if k == 0 else []
                            mm(bank[:, :],
                               wbuf[:, woff + k * D + 128 * j: woff + k * D + 128 * (j + 1)],
                               xfm[:, k * 512:(k + 1) * 512], start=(k == 0),
                               stop=(k == 7), after=aft)
                        wmmq = sT.n
                        we = sop(lambda e, j=j, bank=bank, bias_f=bias_f: e.activation(
                            qtmp[:, j * 512:(j + 1) * 512], bank[:, :], AF.Identity,
                            bias=bias_f[:, j:j + 1], scale=1.0), after=[(sT, wmmq)])
                        wrot = mm(bank2[:, :], PTb[:, :], qtmp[:, j * 512:(j + 1) * 512],
                                  start=True, stop=True, after=[(sS, we)] + guard(bank2))
                        wc = vop(lambda e, bank=bank, bias_f=bias_f, j=j: e.scalar_tensor_tensor(
                            ARF[:, 0:512], bank[:, :], bias_f[:, j:j + 1],
                            scs[:, 0:512], ALU.add, ALU.mult),
                            after=[(sT, wmmq), wsc1, (sS, we)])
                        setguard(bank, sV, wc)
                        wr = vop(lambda e, bank2=bank2: e.tensor_tensor(
                            ARF[:, 512:1024], bank2[:, :], scs[:, 512:1024], ALU.mult),
                            after=[(sT, wrot), wsc2])
                        setguard(bank2, sV, wr)
                        wadd = vop(lambda e, j=j, dsts=dsts: e.tensor_tensor(
                            dsts[:, j * 512:(j + 1) * 512], ARF[:, 0:512],
                            ARF[:, 512:1024], ALU.add), after=[(sV, wr), (sV, wc)])
                    qr_h = d[hdst].rearrange("p (k t) -> p k t", t=S)
                    dma("sync", qr_h[:, :, t0:t0 + STRIP],
                        dsts[:, :].rearrange("p (k t) -> p k t", t=512),
                        after=[(sV, sV.n)])
                barrier()
            if stop in ("qk", "qkcos", "qkrot", "qkstt"):
                return
            for wname, woffD, brow, hdst, do_silu in (
                    ("wv", 2 * D, "bv_r", "h_v", False), ("wg", 2 * D, "bg_r", "h_g", True)):
                w_r = d[wname].rearrange("(k p) m -> p k m", p=128)
                wld = dma("gpsimd", wbuf[:, :].rearrange("p (k m) -> p k m", m=2 * D), w_r,
                          after=[(sT, sT.n)])
                wbr = dma("gpsimd", rowb[:, :], d[brow][:, :])
                vg_spl = [None, None]
                barrier()
                vgx = {0: [(sV, sV.n), (sT, sT.n)], 1: [(sV, sV.n), (sT, sT.n)]}
                for si in range(NSTR):
                    t0 = si * STRIP
                    xv = AR1[:, 4096 * (si % 2):4096 * (si % 2) + 4096]
                    xfm_h = d["h_xfm"].rearrange("p (k t) -> p k t", t=S)
                    wxb = dma("sync", xv.rearrange("p (k t) -> p k t", t=512),
                              xfm_h[:, :, t0:t0 + STRIP], after=vgx[si % 2])
                    for tc in range(4):
                        tok0 = 128 * tc
                        for n in range(4):
                            bank = pb[(tc * 4 + n) % 4]
                            ga = guard(bank)
                            for k in range(8):
                                aft = ([wld, wbr, wxb] + ga) if k == 0 else []
                                mm(bank[:, :],
                                   xv[:, k * 512 + tok0:k * 512 + tok0 + 128],
                                   wbuf[:, k * 2 * D + 512 * n:k * 2 * D + 512 * (n + 1)],
                                   start=(k == 0), stop=False, after=aft)
                            mm(bank[:, :], onesb[:, :], rowb[:, 512 * n:512 * (n + 1)],
                               start=False, stop=True, after=[])
                            fsilu = AF.Silu if do_silu else AF.Identity
                            vb = (si * 4 + tc) % 2
                            we = sop(lambda e, bank=bank, n=n, fsilu=fsilu, vb=vb: e.activation(
                                AR3[:, 2048 * vb + 512 * n:2048 * vb + 512 * (n + 1)],
                                bank[:, :], fsilu), after=[(sT, sT.n), vg_spl[vb]])
                            setguard(bank, sS, we)
                        h_t = d[hdst].rearrange("p (i m) -> p i m", m=2 * D)
                        vb = (si * 4 + tc) % 2
                        vg_spl[vb] = dma("sync", h_t[:, si * 4 + tc, :],
                                         AR3[:, 2048 * vb:2048 * vb + 2048],
                                         after=[(sS, sS.n)])
                    vgx[si % 2] = [(sT, sT.n)]
            if stop == "vg":
                return
            wo_r = d["wo"].rearrange("(k p) m -> p k m", p=128)
            wldo = dma("gpsimd", wbuf[:, 0:16 * D].rearrange("p (k m) -> p k m", m=D),
                       wo_r, after=[(sT, sT.n)])
            wbr2 = dma("gpsimd", rowb2[:, :], d["bo_r"][:, :])
            cb_mix = AR3[:, 4096:6144]
            cb_mixf = AR3[:, 6144:8192]
            cb_kt = AR3[:, 8192:9216]
            cb_qt = AR3[:, 9216:10240]
            sco = AR3[:, 0:2048]
            scguard = {}
            qr_h = d["h_qr"].rearrange("p (k t) -> p k t", t=S)
            kr_h = d["h_kr"].rearrange("p (k t) -> p k t", t=S)
            v_h = d["h_v"].rearrange("p (i m) -> p i m", m=2 * D)
            g_h = d["h_g"].rearrange("p (i m) -> p i m", m=2 * D)
            bufg = {}
            seed = [(sT, sT.n), (sV, sV.n)]
            wsb_prev = 0
            for c in range(NC):
                tok0 = c * CH
                sset = c % 2
                b0 = 6144 * sset
                qrc = AR1[:, b0:b0 + 1024]
                krc = AR1[:, b0 + 1024:b0 + 2048]
                cb_v = AR1[:, b0 + 2048:b0 + 4096]
                cb_g = AR1[:, b0 + 4096:b0 + 6144]
                wq_c = dma("sync", qrc[:, :].rearrange("p (k t) -> p k t", t=128),
                           qr_h[:, :, tok0:tok0 + 128],
                           after=bufg.get(("q", sset), seed))
                wk_c = dma("sync", krc[:, :].rearrange("p (k t) -> p k t", t=128),
                           kr_h[:, :, tok0:tok0 + 128],
                           after=bufg.get(("k", sset), seed))
                wv_c = dma("sync", cb_v, v_h[:, c, :],
                           after=bufg.get(("v", sset), seed))
                wg_c = dma("sync", cb_g, g_h[:, c, :],
                           after=bufg.get(("g", sset), seed))
                if "pieces" in prep_state:
                    prep_p1_step(4)
                wkt = 0
                for j in range(8):
                    bank = pt_b[j % 2]
                    wt = trans(bank[:, 0:128], krc[:, 128 * j:128 * (j + 1)],
                               after=[wk_c] + guard(bank))
                    wkt = vop(lambda e, j=j, bank=bank: e.tensor_tensor(
                        cb_kt[:, 128 * j:128 * (j + 1)], bank[:, 0:128],
                        lamkb[:, 128 * j:128 * (j + 1)], ALU.mult), after=[(sT, wt)])
                    setguard(bank, sV, wkt)
                wqt = 0
                for hp in range(8):
                    wqt = vop(lambda e, hp=hp, qrc=qrc: e.tensor_tensor(
                        cb_qt[:, hp * 128:(hp + 1) * 128],
                        qrc[:, hp * 128:(hp + 1) * 128],
                        lamqb[:, hp * 128:(hp + 1) * 128], ALU.mult),
                        after=[wq_c])
                if stop == "ret1":
                    barrier()
                    continue
                wsc_h = [0] * 16
                for h in range(16):
                    hp, half = h // 2, h % 2
                    sbk = pb[h // 4]
                    sl = sbk[:, 128 * (h % 4):128 * (h % 4) + 128]
                    sga = guard(sbk) if h % 4 == 0 else []
                    wsc_h[h] = mm(sl,
                                  krc[64 * half:64 * half + 64, hp * 128:(hp + 1) * 128],
                                  qrc[64 * half:64 * half + 64, hp * 128:(hp + 1) * 128],
                                  start=True, stop=True, after=[wq_c, wk_c] + sga)
                last_wsc = wsc_h[15]
                wmask_h = [0] * 16
                for h in range(16):
                    sbk = pb[h // 4]
                    sl = sbk[:, 128 * (h % 4):128 * (h % 4) + 128]
                    wmask_h[h] = vop(lambda e, h=h, sl=sl: e.tensor_tensor(
                        sco[:, h * 128:(h + 1) * 128], sl,
                        DTb[:, h * CH:(h + 1) * CH], ALU.mult),
                        after=[(sT, wsc_h[(h // 4) * 4 + 3])])
                setguard(pb[0], sV, wmask_h[3])
                setguard(pb[1], sV, wmask_h[7])
                if stop == "ret2s":
                    barrier()
                    continue
                wapply_all = []
                grp_meta = []
                wsq = 0
                for gi, grp in enumerate(GRPS):
                    bank = pb[gi]
                    ga = guard(bank)
                    first = True
                    for idx, h in enumerate(grp):
                        hp, half = h // 2, h % 2
                        col = idx * 130
                        aft1 = ([(sV, wmask_h[15]), wv_c] + ga) if first \
                            else [(sV, wmask_h[h]), wv_c]
                        mm(bank[:, col:col + 128], sco[:, h * 128:(h + 1) * 128],
                           cb_v[:, h * 128:(h + 1) * 128], start=first, stop=False,
                           after=aft1)
                        first = False
                        mm(bank[:, col + 128:col + 129], sco[:, h * 128:(h + 1) * 128],
                           onesc[:, :], start=False, stop=False, after=[])
                        mm(bank[:, col:col + 128],
                           cb_qt[64 * half:64 * half + 64, hp * 128:(hp + 1) * 128],
                           Sb[64 * half:64 * half + 64, hp * 130:hp * 130 + 128],
                           start=False, stop=False,
                           after=[(sV, wqt), (sV, wsb_prev)])
                        mm(bank[:, col + 128:col + 129],
                           cb_qt[64 * half:64 * half + 64, hp * 128:(hp + 1) * 128],
                           Sb[64 * half:64 * half + 64, hp * 130 + 128:hp * 130 + 129],
                           start=False, stop=(idx == len(grp) - 1), after=[])
                    wmm_grp = sT.n
                    gsz = len(grp)
                    view3 = bank[:, 0:gsz * 130].rearrange("p (a b) -> p a b", b=130)
                    h0 = grp[0]
                    wmu = vop(lambda e, view3=view3, h0=h0, gsz=gsz: e.tensor_reduce(
                        epi[:, h0:h0 + gsz], view3[:, :, 0:128], axis=AX.X, op=ALU.add),
                        after=[(sT, wmm_grp)])
                    wrs = vop(lambda e, view3=view3, h0=h0, gsz=gsz: e.tensor_copy(
                        epi[:, 16 + h0:16 + h0 + gsz],
                        view3[:, :, 128:129].rearrange("p a b -> p (a b)")),
                        after=[(sT, wmm_grp)])
                    for idx, h in enumerate(grp):
                        wsq = sop(lambda e, idx=idx, h=h, bank=bank: e.activation(
                            junk[:, :], bank[:, idx * 130:idx * 130 + 128], AF.Square,
                            accum_out=epi[:, 32 + h:32 + h + 1]),
                            after=[(sT, wmm_grp), (sV, wrs), (sV, wmu)])
                    grp_meta.append((bank, grp, wmu, wrs))
                if stop == "ret2n":
                    barrier()
                    continue
                wstat = sV.n
                e_mu = epi[:, 0:16]
                e_rs = epi[:, 16:32]
                e_sq = epi[:, 32:48]
                e_scale = epi[:, 48:64]
                e_A = epi[:, 64:80]
                e_B = epi[:, 80:96]
                e_t1 = epi[:, 96:112]
                e_t2 = epi[:, 112:128]
                c1 = sop(lambda e, e_t1=e_t1, e_rs=e_rs: e.activation(
                    e_t1, e_rs, AF.Abs), after=[(sV, wstat)])
                c2 = vop(lambda e, c=c, e_t1=e_t1: e.tensor_tensor(
                    e_t1, e_t1, sqrtZb[:, c * 16:c * 16 + 16],
                    ALU.max), after=[(sS, c1)])
                c3 = vop(lambda e, e_scale=e_scale, e_t1=e_t1: e.reciprocal(
                    e_scale, e_t1), after=[(sV, c2)])
                c4 = vop(lambda e, e_mu=e_mu: e.tensor_single_scalar(
                    e_mu, e_mu, 1.0 / HD, ALU.mult), after=[(sV, wstat)])
                c5 = vop(lambda e, e_t1=e_t1, e_mu=e_mu: e.tensor_tensor(
                    e_t1, e_mu, e_mu, ALU.mult), after=[(sV, c4), (sV, c3)])
                c6 = vop(lambda e, e_t2=e_t2, e_sq=e_sq, e_t1=e_t1: e.scalar_tensor_tensor(
                    e_t2, e_sq, 1.0 / HD, e_t1, ALU.mult, ALU.subtract),
                    after=[(sS, wsq), (sV, c5)])
                c7 = vop(lambda e, e_t1=e_t1, e_scale=e_scale: e.tensor_tensor(
                    e_t1, e_scale, e_scale, ALU.mult), after=[(sV, c6)])
                c8 = vop(lambda e, e_t2=e_t2, e_t1=e_t1: e.tensor_tensor(
                    e_t2, e_t2, e_t1, ALU.mult), after=[(sV, c7)])
                c9 = sop(lambda e, e_t2=e_t2: e.activation(
                    e_t2, e_t2, AF.Sqrt, bias=EPS, scale=1.0), after=[(sV, c8)])
                c10 = vop(lambda e, e_t1=e_t1, e_t2=e_t2: e.reciprocal(
                    e_t1, e_t2), after=[(sS, c9)])
                c11 = vop(lambda e, e_A=e_A, e_scale=e_scale, e_t1=e_t1: e.tensor_tensor(
                    e_A, e_scale, e_t1, ALU.mult), after=[(sV, c10)])
                c12 = vop(lambda e, e_B=e_B, e_mu=e_mu, e_A=e_A: e.scalar_tensor_tensor(
                    e_B, e_mu, -1.0, e_A, ALU.mult, ALU.mult), after=[(sV, c11)])
                if stop == "ret2k":
                    barrier()
                    continue
                wap_last = 0
                for bank, grp, _, _ in grp_meta:
                    wap = 0
                    for idx, h in enumerate(grp):
                        wap = sop(lambda e, bank=bank, idx=idx, h=h: e.activation(
                            cb_mix[:, h * 128:(h + 1) * 128],
                            bank[:, idx * 130:idx * 130 + 128], AF.Identity,
                            bias=epi[:, 80 + h:80 + h + 1],
                            scale=epi[:, 64 + h:64 + h + 1]),
                            after=[(sV, c12)])
                    setguard(bank, sS, wap)
                    wap_last = wap
                if stop in ("ret2", "ret2a", "ret2b", "ret2c", "ret2d", "ret2e", "ret2f"):
                    barrier()
                    continue
                wstt_all = 0
                for hp in range(8):
                    bankA, bankB = pb[2 + 2 * (hp % 2)], pb[3 + 2 * (hp % 2)]
                    gaA, gaB = guard(bankA), guard(bankB)
                    h0, h1 = 2 * hp, 2 * hp + 1
                    mm(bankA[:, 0:128], cb_kt[:, hp * 128:(hp + 1) * 128],
                       cb_v[:, h0 * 128:(h0 + 1) * 128], start=True, stop=False,
                       after=[(sV, wkt), wv_c] + gaA)
                    mm(bankA[:, 128:129], cb_kt[:, hp * 128:(hp + 1) * 128], onesc[:, :],
                       start=False, stop=True, after=[])
                    mm(bankB[:, 0:128], cb_kt[:, hp * 128:(hp + 1) * 128],
                       cb_v[:, h1 * 128:(h1 + 1) * 128], start=True, stop=False,
                       after=gaB)
                    mm(bankB[:, 128:129], cb_kt[:, hp * 128:(hp + 1) * 128], onesc[:, :],
                       start=False, stop=True, after=[])
                    wd = sT.n
                    wsA = vop(lambda e, hp=hp, bankA=bankA: e.scalar_tensor_tensor(
                        Sf[0:64, hp * 130:hp * 130 + 129],
                        Sf[0:64, hp * 130:hp * 130 + 129],
                        lam128b[0:64, hp:hp + 1], bankA[0:64, 0:129],
                        ALU.mult, ALU.add), after=[(sT, wd)])
                    wsB = vop(lambda e, hp=hp, bankB=bankB: e.scalar_tensor_tensor(
                        Sf[64:128, hp * 130:hp * 130 + 129],
                        Sf[64:128, hp * 130:hp * 130 + 129],
                        lam128b[64:128, hp:hp + 1], bankB[64:128, 0:129],
                        ALU.mult, ALU.add), after=[(sT, wd)])
                    setguard(bankA, sV, wsA)
                    setguard(bankB, sV, wsB)
                    wstt_all = wsB
                st_done = sT.n
                wsb = vop(lambda e: e.tensor_copy(Sb[:, :], Sf[:, :]),
                          after=[(sT, sT.n), (sV, wstt_all)])
                wsb_prev = wsb
                bufg[("q", sset)] = [(sT, last_wsc), (sV, wqt)]
                bufg[("k", sset)] = [(sT, last_wsc)]
                bufg[("v", sset)] = [(sT, st_done)]
                if dbg == 2:
                    dma("sync", d["h_st"].rearrange("p (cc m) -> p cc m", m=8 * 130)[:, c, :],
                        Sf[:, :], after=[(sV, wsb)])
                if stop == "ret3":
                    barrier()
                    continue
                wgm = vop(lambda e, cb_g=cb_g: e.tensor_tensor(
                    cb_mix, cb_mix, cb_g, ALU.mult), after=[(sS, wap_last), wg_c])
                bufg[("g", sset)] = [(sV, wgm)]
                if stop == "ret5":
                    barrier()
                    continue
                if dbg == 2:
                    dma("sync", d["h_mix"].rearrange("p (c m) -> p c m", m=2 * D)[:, c, :],
                        cb_mix, after=[(sV, wgm)])
                    dma("sync", d["h_epi"].rearrange("p (c m) -> p c m", m=128)[:, c, :],
                        epi[:, :], after=[(sV, sV.n), (sS, sS.n)])
                wtr = 0
                for j in range(16):
                    bank = pt_b[j % 2]
                    wt = trans(bank[:, 0:128], cb_mix[:, 128 * j:128 * (j + 1)],
                               after=[(sV, wgm)] + guard(bank))
                    wtr = vop(lambda e, j=j, bank=bank: e.tensor_copy(
                        cb_mixf[:, 128 * j:128 * (j + 1)], bank[:, 0:128]),
                        after=[(sT, wt)])
                    setguard(bank, sV, wtr)
                for n in range(2):
                    bank = pb[4 + n]
                    ga = guard(bank)
                    for k in range(16):
                        aft = ([(sV, wtr), wldo, wbr2] + ga) if k == 0 else []
                        mm(bank[:, :], cb_mixf[:, 128 * k:128 * (k + 1)],
                           wbuf[:, k * D + 512 * n:k * D + 512 * (n + 1)],
                           start=(k == 0), stop=False, after=aft)
                    mm(bank[:, :], onesb[:, :], rowb2[:, 512 * n:512 * (n + 1)],
                       start=False, stop=True, after=[])
                    wr = vop(lambda e, bank=bank, c=c, n=n: e.tensor_tensor(
                        R[:, c * D + 512 * n:c * D + 512 * (n + 1)],
                        R[:, c * D + 512 * n:c * D + 512 * (n + 1)], bank[:, :],
                        ALU.add), after=[(sT, sT.n)])
                    setguard(bank, sV, wr)
            barrier()


        JUNK1024 = AR3[:, 11264:12288]

        def emit_ffn_prep():
            if "pieces" not in prep_state:
                prep_p1_init()
            prep_p1_step(len(prep_state["pieces"]))
            pieces = prep_state["pieces"]
            red = prep_state["red"]
            lanes = prep_state["lanes"]
            w_r1 = vop(lambda e: e.tensor_reduce(sbB[:, 40:41], stat[:, 0:32],
                                                 axis=AX.X, op=ALU.add),
                       after=[(sV, max(red))])
            w_r2 = vop(lambda e: e.tensor_reduce(sbB[:, 41:42], stat[:, 32:64],
                                                 axis=AX.X, op=ALU.add), after=[])
            ga = guard(pb[0])
            wm1 = mm(pb[0][0:1, 0:1], sbB[:, 40:41], onescf[:, :], start=True, stop=True,
                     after=[(sV, w_r1), (sV, w_r2)] + ga)
            wm2 = mm(pb[0][0:1, 1:2], sbB[:, 41:42], onescf[:, :], start=False, stop=True,
                     after=[])
            wt1 = vop(lambda e: e.tensor_scalar(sbB[0:1, 8:10], pb[0][0:1, 0:2],
                                                1.0 / (DH * D), 1e-5, ALU.mult, ALU.max),
                      after=[(sT, wm2)])
            setguard(pb[0], sV, wt1)
            wt2 = vop(lambda e: e.reciprocal(sbB[0:1, 10:12], sbB[0:1, 8:10]),
                      after=[(sV, wt1)])
            ga = guard(pb[1])
            wm3 = mm(pb[1][:, 0:4], onesrf[0:1, :], sbB[0:1, 8:12], start=True, stop=True,
                     after=[(sV, wt2)] + ga)
            wb4 = vop(lambda e: e.tensor_copy(wsc[:, 0:4], pb[1][:, 0:4]),
                      after=[(sT, wm3)])
            setguard(pb[1], sV, wb4)
            wcs = vop(lambda e: e.tensor_scalar(convwsc[:, :], convwb[:, :],
                                                wsc[:, 0:1], None, ALU.mult),
                      after=[(sV, wb4)] + dall("gpsimd"))
            lanes2 = lanes + [DTb[:, 0:1024], DTb[:, 1024:2048]]
            cvt = [wcs] * 5
            spl = [None] * 5
            for pi, (wn, srcap, dstap, idx) in enumerate(pieces):
                hb = pi % 5
                buf = lanes2[hb]
                qout = AR3[:, 1024 * hb:1024 * (hb + 1)]
                wld = dma("sync" if pi % 2 == 0 else "gpsimd", buf, srcap,
                          after=[(sV, cvt[hb])])
                swcol = 2 if wn == "w1" else 3
                wsp = sop(lambda e, swcol=swcol, buf=buf: e.activation(
                    buf, buf, AF.Identity, scale=wsc[:, swcol:swcol + 1]),
                    after=[wld, (sV, wcs)])
                wi = vop(lambda e, buf=buf: e.tensor_copy(i32s[:, :], buf),
                         after=[(sS, wsp)])
                cvt[hb] = wi
                wcl = vop(lambda e, qout=qout: e.tensor_scalar(
                    qout, i32s[:, :], -1.0, 1.0, ALU.max, ALU.min),
                    after=[(sV, wi), spl[hb]])
                spl[hb] = dma("sync" if pi % 2 == 0 else "gpsimd", dstap, qout,
                              after=[(sV, wcl)])
            barrier()

        def emit_ffn():
            xqfm = AR1[:, :]
            macc = AR3[:, 0:2048]
            mincc = AR3[:, 2048:4096]
            yring = ([AR3[:, 4096 + 256 * i:4096 + 256 * (i + 1)] for i in range(2)]
                     + [AR3[:, 6144 + 256 * i:6144 + 256 * (i + 1)] for i in range(2)])
            pring = ([AR3[:, 4608 + 272 * i:4608 + 272 * i + 258] for i in range(2)]
                     + [AR3[:, 6656 + 272 * i:6656 + 272 * i + 258] for i in range(2)])
            hsring = ([AR3[:, 5152 + 256 * i:5152 + 256 * (i + 1)] for i in range(2)]
                      + [AR3[:, 7200 + 256 * i:7200 + 256 * (i + 1)] for i in range(2)])
            hload = AR3[:, 6144:8192]
            hqb = AR3[:, 8192:10240]
            sbcols = AR3[:, 10496:10512]
            for i in range(NT):
                wa = emit_ln(i, ARF[:, 0:1024], after_r=wait_init, stcol=32 + (i % 2) * 2)
                wmx = vop(lambda e: e.tensor_reduce(sbB[:, 16:17], ARF[:, 0:1024],
                                                    axis=AX.X, op=ALU.max,
                                                    apply_absolute_value=True),
                          after=[(sS, wa)])
                wcl = vop(lambda e: e.tensor_scalar(sbB[:, 17:18], sbB[:, 16:17],
                                                    1e-5, 1.0 / 127.0, ALU.max, ALU.mult),
                          after=[(sV, wmx)])
                wrc = vop(lambda e: e.reciprocal(sbB[:, 18:19], sbB[:, 17:18]),
                          after=[(sV, wcl)])
                wr1 = vop(lambda e: e.tensor_scalar(ARF[:, 1024:2048], ARF[:, 0:1024],
                                                    sbB[:, 18:19], None, ALU.mult),
                          after=[(sV, wrc)])
                wi3 = vop(lambda e: e.tensor_copy(i32s[:, 0:1024], ARF[:, 1024:2048]),
                          after=[(sV, wr1)])
                sg = stg[:, (i % 2) * D:((i % 2) + 1) * D]
                wxq = vop(lambda e, sg=sg: e.tensor_scalar(
                    sg, i32s[:, 0:1024], sbB[:, 17:18], None, ALU.mult),
                    after=[(sV, wi3)])
                p.wait("tensor", [(sV, wxq)])
                emit_transpose_tile(i, sg, xqfm, 2048, [(sV, wxq)])
            barrier()
            winit1 = vop(lambda e: e.memset(macc[:, :], -3.0e38))
            winit2 = vop(lambda e: e.memset(mincc[:, :], 3.0e38))
            h_hs_v = d["h_hs"].rearrange("p (tc kk j) -> p tc kk j", kk=32, j=128)
            for half in range(2):
                wq1 = dma("gpsimd",
                          wbuf[:, :].rearrange("p (k m) -> p k m", m=2048),
                          d["h_w1q"].rearrange("p (k m) -> p k m", m=4096)[:, :, 2048 * half:2048 * (half + 1)],
                          after=[(sT, sT.n)])
                spl4 = [None] * 4
                for win in range(8):
                    a = win * 256 - 1
                    lo, hi = max(a, 0), min(a + 258, S)
                    off = lo - a
                    for c16 in range(16):
                        kk = half * 16 + c16
                        bank = pb[c16 % 4]
                        ga = guard(bank)
                        for k in range(8):
                            aft = ([wq1, (sV, winit2)] + ga) if k == 0 else []
                            mm(bank[:, off:off + hi - lo],
                               wbuf[:, k * 2048 + c16 * 128:k * 2048 + c16 * 128 + 128],
                               xqfm[:, k * 2048 + lo:k * 2048 + hi],
                               start=(k == 0), stop=(k == 7), after=aft)
                        wmm = sT.n
                        wfix = 0
                        if off == 1:
                            wfix = vop(lambda e, bank=bank: e.memset(bank[:, 0:1], 0.0),
                                       after=[(sT, wmm)])
                        elif hi - lo < 258:
                            wfix = vop(lambda e, bank=bank: e.memset(bank[:, 257:258], 0.0),
                                       after=[(sT, wmm)])
                        pr = pring[c16 % 4]
                        wev = sop(lambda e, bank=bank, pr=pr: e.activation(
                            pr, bank[:, 0:258], AF.Identity), after=[(sT, wmm), (sV, wfix)])
                        setguard(bank, sS, wev)
                        yr = yring[c16 % 4]
                        wt1 = vop(lambda e, pr=pr, yr=yr, kk=kk: e.tensor_scalar(
                            yr, pr[:, 1:257], convwsc[:, 3 * kk + 1:3 * kk + 2], None,
                            ALU.mult), after=[(sS, wev)])
                        wt2 = vop(lambda e, pr=pr, yr=yr, kk=kk: e.scalar_tensor_tensor(
                            yr, pr[:, 0:256], convwsc[:, 3 * kk:3 * kk + 1], yr,
                            ALU.mult, ALU.add), after=[(sV, wt1)])
                        wt3 = vop(lambda e, pr=pr, yr=yr, kk=kk: e.scalar_tensor_tensor(
                            yr, pr[:, 2:258], convwsc[:, 3 * kk + 2:3 * kk + 3], yr,
                            ALU.mult, ALU.add), after=[(sV, wt2)])
                        hs = hsring[c16 % 4]
                        wsl = sop(lambda e, yr=yr, hs=hs, kk=kk: e.activation(
                            hs, yr, AF.Silu, bias=convbb[:, kk:kk + 1], scale=1.0),
                            after=[(sV, wt3), spl4[c16 % 4]])
                        wmx = vop(lambda e, hs=hs, win=win: e.tensor_tensor(
                            macc[:, win * 256:(win + 1) * 256],
                            macc[:, win * 256:(win + 1) * 256], hs, ALU.max),
                            after=[(sS, wsl), (sV, winit1)])
                        wmn = vop(lambda e, hs=hs, win=win: e.tensor_tensor(
                            mincc[:, win * 256:(win + 1) * 256],
                            mincc[:, win * 256:(win + 1) * 256], hs, ALU.min),
                            after=[(sS, wsl), (sV, winit2)])
                        spl4[c16 % 4] = dma("sync",
                                            h_hs_v[:, 2 * win:2 * win + 2, kk, :],
                                            hs.rearrange("p (a j) -> p a j", j=128),
                                            after=[(sS, wsl), (sV, wmx), (sV, wmn)])
                barrier()
            for i in range(NT):
                bk = pt_b[i % 2]
                wtr = trans(bk[:, 0:128], macc[:, i * 128:(i + 1) * 128],
                            after=[(sV, sV.n), (sG, sG.n)] + guard(bk))
                wtr2 = trans(bk[:, 128:256], mincc[:, i * 128:(i + 1) * 128], after=[])
                wmx = vop(lambda e, bk=bk: e.tensor_reduce(
                    sbB[:, 20:21], bk[:, 0:128], axis=AX.X, op=ALU.max), after=[(sT, wtr2)])
                wmn = vop(lambda e, bk=bk: e.tensor_reduce(
                    sbB[:, 21:22], bk[:, 128:256], axis=AX.X, op=ALU.min), after=[(sT, wtr2)])
                setguard(bk, sV, max(wmx, wmn))
                wng = vop(lambda e: e.tensor_scalar(sbB[:, 21:22], sbB[:, 21:22], -1.0,
                                                    None, ALU.mult), after=[(sV, wmn)])
                wmm2 = vop(lambda e: e.tensor_tensor(sbB[:, 20:21], sbB[:, 20:21],
                                                     sbB[:, 21:22], ALU.max),
                           after=[(sV, wng), (sV, wmx)])
                wcl = vop(lambda e: e.tensor_scalar(sbB[:, 20:21], sbB[:, 20:21], 1e-5,
                                                    1.0 / 127.0, ALU.max, ALU.mult),
                          after=[(sV, wmm2)])
                wrc = vop(lambda e: e.reciprocal(sbB[:, 22:23], sbB[:, 20:21]),
                          after=[(sV, wcl)])
                wsb_ = vop(lambda e, i=i: e.tensor_copy(sbcols[:, i:i + 1], sbB[:, 22:23]),
                           after=[(sV, wrc)])
                winv = vop(lambda e, i=i: e.reciprocal(ffs[:, i:i + 1], sbcols[:, i:i + 1]),
                           after=[(sV, wsb_)])
                bk2 = pt_b[(i + 1) % 2]
                wtr = trans(bk2[0:1, 0:128], sbcols[:, i:i + 1],
                            after=[(sV, wsb_)] + guard(bk2))
                wsr = vop(lambda e, bk2=bk2, i=i: e.tensor_copy(
                    srowb[0:1, i * 128:(i + 1) * 128], bk2[0:1, 0:128]),
                    after=[(sT, wtr)])
                setguard(bk2, sV, wsr)
            if dbg == 2:
                dma("sync", d["h_ab"][:, 0:16], ffs[:, 0:16], after=[(sV, sV.n)])
            barrier()
            for half in range(2):
                wq2 = dma("gpsimd", wbuf[:, :],
                          d["h_w2q"][:, half * 16 * 1024:(half + 1) * 16 * 1024],
                          after=[(sT, sT.n)])
                prev_q = 0
                prev_r1 = 0
                for tc in range(NT):
                    whl = dma("sync", hload,
                              d["h_hs"][:, tc * 4096 + half * 2048:tc * 4096 + (half + 1) * 2048],
                              after=[(sV, prev_r1)])
                    gb = guard(pb[4])
                    wbc = mm(pb[4][:, 0:128], onesb[0:1, :],
                             srowb[0:1, tc * 128:(tc + 1) * 128],
                             start=True, stop=True, after=[(sV, sV.n)] + gb)
                    _h3 = hload.rearrange("p (a j) -> p a j", j=128)
                    _s3 = pb[4][:, 0:128].rearrange("p (a j) -> p a j", j=128)
                    _h3b, _s3b = broadcast_tensor_aps(_h3, _s3)
                    wr1 = vop(lambda e, _h3b=_h3b, _s3b=_s3b: e.tensor_tensor(
                        ARF[:, :].rearrange("p (a j) -> p a j", j=128),
                        _h3b, _s3b,
                        ALU.mult), after=[(sT, wbc), whl])
                    prev_r1 = wr1
                    setguard(pb[4], sV, wr1)
                    for hh in range(2):
                        wi3 = vop(lambda e, hh=hh: e.tensor_copy(
                            i32s[:, :], ARF[:, 1024 * hh:1024 * (hh + 1)]),
                            after=[(sV, wr1)])
                        prev_q = vop(lambda e, hh=hh: e.tensor_copy(
                            hqb[:, 1024 * hh:1024 * (hh + 1)], i32s[:, :]),
                            after=[(sV, wi3)])
                    if dbg == 2:
                        dma("sync", d["h_hq"][:, (half * NT + tc) * 2048:(half * NT + tc + 1) * 2048],
                            hqb, after=[(sV, prev_q)])
                    for n in range(2):
                        bank = pb[n]
                        ga = guard(bank)
                        for k16 in range(16):
                            aft = ([(sV, prev_q), wq2] + ga) if k16 == 0 else []
                            mm(bank[:, :], hqb[:, k16 * 128:(k16 + 1) * 128],
                               wbuf[:, k16 * 1024 + 512 * n:k16 * 1024 + 512 * (n + 1)],
                               start=(k16 == 0), stop=(k16 == 15), after=aft)
                        wsc_ = vop(lambda e, tc=tc: e.tensor_scalar(
                            sbB[:, 24:25], ffs[:, tc:tc + 1], wsc[:, 1:2], None,
                            ALU.mult), after=[(sT, sT.n)])
                        wra = vop(lambda e, bank=bank, tc=tc, n=n: e.scalar_tensor_tensor(
                            R[:, tc * D + 512 * n:tc * D + 512 * (n + 1)], bank[:, :],
                            sbB[:, 24:25], R[:, tc * D + 512 * n:tc * D + 512 * (n + 1)],
                            ALU.mult, ALU.add), after=[(sV, wsc_)])
                        setguard(bank, sV, wra)
                barrier()

        def emit_attn_b():
            STRIP = 512
            NSTR = S // STRIP
            xfmb = AR3[:, 0:4096]
            valfm = AR1[:, :]
            astrip = AR3[:, 4096:8192]
            svstrip = AR3[:, 8192:12288]
            carries = sbB[:, 32:40]
            aring = [ARF[:, 512 * i:512 * (i + 1)] for i in range(2)]
            winr = d["w_inT"].rearrange("(k p) m -> p k m", p=128)
            wldin = dma("gpsimd", wbuf[:, :].rearrange("p (k m) -> p k m", m=2 * D),
                        winr, after=[(sT, sT.n)])
            wcar = vop(lambda e: e.memset(carries[:, :], 0.0))
            h_ab_v = d["h_ab"].rearrange("p (j t) -> p j t", t=S)
            prev_asp = None
            for si in range(NSTR):
                t0 = si * STRIP
                w_xfm = 0
                for il in range(4):
                    i = si * 4 + il
                    sg = stg[:, (il % 2) * D:((il % 2) + 1) * D]
                    wa = emit_ln(i, sg, after_r=wait_init, stcol=2 * i, st=lnst)
                    w_xfm = emit_transpose_tile(il, sg, xfmb, 512, [(sS, wa)])
                for j in range(16):
                    bank = pb[j % 4]
                    ga = guard(bank)
                    for k in range(8):
                        aft = ([wldin, (sV, w_xfm)] + ga) if k == 0 else []
                        mm(bank[:, :], wbuf[:, k * 2 * D + 128 * j:k * 2 * D + 128 * (j + 1)],
                           xfmb[:, k * 512:(k + 1) * 512], start=(k == 0), stop=(k == 7),
                           after=aft)
                    wmm = sT.n
                    if j < 8:
                        ar = aring[j % 2]
                        wev = sop(lambda e, bank=bank, j=j, ar=ar: e.activation(
                            ar, bank[:, :], AF.Sigmoid, bias=nbin_fb[:, j:j + 1],
                            scale=-1.0), after=[(sT, wmm), prev_asp])
                        setguard(bank, sS, wev)
                        prev_asp = dma("sync", h_ab_v[:, j, t0:t0 + STRIP], ar,
                                       after=[(sS, wev)])
                    else:
                        jj = j - 8
                        wev = sop(lambda e, bank=bank, j=j, jj=jj, t0=t0: e.activation(
                            valfm[:, jj * 2048 + t0:jj * 2048 + t0 + STRIP], bank[:, :],
                            AF.Identity, bias=bin_fb[:, j:j + 1], scale=1.0),
                            after=[(sT, wmm)])
                        setguard(bank, sS, wev)
            barrier()
            wsr_ = d["w_sT"].rearrange("(k p) m -> p k m", p=128)
            wor_ = d["w_oT"].rearrange("(k p) m -> p k m", p=128)
            wlds = dma("gpsimd", wbuf[:, 0:8 * D].rearrange("p (k m) -> p k m", m=D),
                       wsr_, after=[(sT, sT.n)])
            wldo = dma("gpsimd", wbuf[:, 8 * D:16 * D].rearrange("p (k m) -> p k m", m=D),
                       wor_, after=[(sT, sT.n)])
            wbr2 = dma("gpsimd", rowb2[:, :], d["bo2_r"][:, :])
            prev_wr = [0]
            for si in range(NSTR):
                t0 = si * STRIP
                wla = dma("gpsimd", astrip[:, :].rearrange("p (j t) -> p j t", t=512),
                          h_ab_v[:, :, t0:t0 + STRIP], after=[(sV, sV.n)])
                for j in range(8):
                    bank = pb[4 + j % 2]
                    ga = guard(bank)
                    for k in range(8):
                        aft = ([wlds, (sT, sT.n)] + ga) if k == 0 else []
                        mm(bank[:, :], wbuf[:, k * D + 128 * j:k * D + 128 * (j + 1)],
                           valfm[:, k * 2048 + t0:k * 2048 + t0 + STRIP],
                           start=(k == 0), stop=(k == 7), after=aft)
                    wev = sop(lambda e, bank=bank, j=j: e.activation(
                        svstrip[:, j * 512:(j + 1) * 512], bank[:, :], AF.Tanh,
                        bias=bs_fb[:, j:j + 1], scale=1.0), after=[(sT, sT.n)])
                    setguard(bank, sS, wev)
                wsv = sS.n
                wsc_l = 0
                for j in range(8):
                    sl = slice(j * 512, (j + 1) * 512)
                    wt = vop(lambda e, sl=sl: e.tensor_tensor(
                        stg[:, 0:512], astrip[:, sl], svstrip[:, sl], ALU.mult),
                        after=[(sS, wsv), wla])
                    wb_ = vop(lambda e, sl=sl: e.tensor_tensor(
                        svstrip[:, sl], svstrip[:, sl], stg[:, 0:512], ALU.subtract),
                        after=[(sV, wt)])
                    wscan = vop(lambda e, sl=sl, j=j, si=si: e.tensor_tensor_scan(
                        svstrip[:, sl], astrip[:, sl], svstrip[:, sl],
                        0.0 if si == 0 else carries[:, j:j + 1],
                        ALU.mult, ALU.add), after=[(sV, wb_), (sV, wcar)])
                    wsc_l = vop(lambda e, sl=sl, j=j: e.tensor_copy(
                        carries[:, j:j + 1], svstrip[:, j * 512 + 511:j * 512 + 512]),
                        after=[(sV, wscan)])
                for tc in range(4):
                    i = si * 4 + tc
                    wx3 = sop(lambda e, i=i: e.activation(
                        ARF[:, 1024:2048], R[:, i * D:(i + 1) * D], AF.Identity,
                        bias=lnst[:, 2 * i:2 * i + 1], scale=lnst[:, 2 * i + 1:2 * i + 2]),
                        after=[(sV, wsc_l), (sV, prev_wr[0])])
                    for n in range(2):
                        bank = pb[n]
                        ga = guard(bank)
                        for k in range(8):
                            aft = ([wldo, wbr2, (sV, wsc_l)] + ga) if k == 0 else []
                            mm(bank[:, :],
                               svstrip[:, k * 512 + tc * 128:k * 512 + tc * 128 + 128],
                               wbuf[:, 8 * D + k * D + 512 * n:8 * D + k * D + 512 * (n + 1)],
                               start=(k == 0), stop=False, after=aft)
                        mm(bank[:, :], onesb[:, :], rowb2[:, 512 * n:512 * (n + 1)],
                           start=False, stop=True, after=[])
                        wadd = vop(lambda e, bank=bank, i=i, n=n: e.tensor_tensor(
                            ARF[:, 1024 + 512 * n:1024 + 512 * (n + 1)],
                            ARF[:, 1024 + 512 * n:1024 + 512 * (n + 1)], bank[:, :],
                            ALU.add), after=[(sT, sT.n), (sS, wx3)])
                        setguard(bank, sV, wadd)
                        wr_ = vop(lambda e, i=i, n=n: e.tensor_tensor(
                            R[:, i * D + 512 * n:i * D + 512 * (n + 1)],
                            R[:, i * D + 512 * n:i * D + 512 * (n + 1)],
                            ARF[:, 1024 + 512 * n:1024 + 512 * (n + 1)], ALU.add),
                            after=[(sV, wadd)])
                        prev_wr[0] = wr_
            barrier()

        ffn_prepped = [False]
        if any(p.startswith("ffn") for p in parts):
            prep_p1_init()
        for pi_, prt in enumerate(parts):
            if prt.startswith("attn_a"):
                emit_attn_a(stop=prt.split(":")[1] if ":" in prt else None)
            elif prt.startswith("ffn"):
                if not ffn_prepped[0]:
                    emit_ffn_prep()
                    ffn_prepped[0] = True
                emit_ffn()
            elif prt.startswith("attn_b"):
                emit_attn_b()
            if dbg == 2 and pi_ < 4:
                dma("sync", d[f"h_R{pi_}"][:, :], R[:, :],
                    after=[(sV, sV.n), (sS, sS.n), (sT, sT.n), (sG, sG.n)])
                barrier()

        out_r = d["out"].rearrange("(i p) dd -> p i dd", p=128)
        wst = dma("sync", out_r, R_r, after=[(sV, sV.n), (sS, sS.n), (sT, sT.n),
                                             (sG, sG.n)])
        p.wait("sync", [wst])

        for eng_name in ENGINES:
            fns = list(p.streams[eng_name])

            def body(eng, fns=fns):
                for f in fns:
                    f(eng)
            getattr(blk, eng_name)(body)

    return nc, d


def host_inputs(inputs):
    g = dict(inputs)
    out = {}
    out["wq"] = np.ascontiguousarray(g["L0_attn_a_Wq"], np.float32)
    out["wk"] = np.ascontiguousarray(g["L0_attn_a_Wk"], np.float32)
    out["wv"] = np.ascontiguousarray(g["L0_attn_a_Wv"], np.float32)
    out["wg"] = np.ascontiguousarray(g["L0_attn_a_Wg"], np.float32)
    out["wo"] = np.ascontiguousarray(g["L0_attn_a_Wo"], np.float32)
    out["bq_f"] = np.ascontiguousarray(g["L0_attn_a_bq"].reshape(8, 128).T, np.float32)
    out["bk_f"] = np.ascontiguousarray(g["L0_attn_a_bk"].reshape(8, 128).T, np.float32)
    out["bv_r"] = np.ascontiguousarray(g["L0_attn_a_bv"].reshape(1, 2 * D), np.float32)
    out["bg_r"] = np.ascontiguousarray(g["L0_attn_a_bg"].reshape(1, 2 * D), np.float32)
    out["bo_r"] = np.ascontiguousarray(g["L0_attn_a_bo"].reshape(1, D), np.float32)
    out["w_inT"] = np.ascontiguousarray(np.asarray(g["L0_attn_b_W_in"]).T, np.float32)
    out["bin_f"] = np.ascontiguousarray(g["L0_attn_b_b_in"].reshape(16, 128).T, np.float32)
    out["w_sT"] = np.ascontiguousarray(np.asarray(g["L0_attn_b_W_s"]).T, np.float32)
    out["bs_f"] = np.ascontiguousarray(g["L0_attn_b_b_s"].reshape(8, 128).T, np.float32)
    out["w_oT"] = np.ascontiguousarray(np.asarray(g["L0_attn_b_W_o"]).T, np.float32)
    out["bo2_r"] = np.ascontiguousarray(g["L0_attn_b_b_o"].reshape(1, D), np.float32)
    out["w1T"] = np.ascontiguousarray(np.asarray(g["L0_ffn_w1"]).T, np.float32)
    out["w2T"] = np.ascontiguousarray(np.asarray(g["L0_ffn_w2"]).T, np.float32)
    cw = np.asarray(g["L0_ffn_conv_w"]).reshape(DH, 3)
    cwt = cw.reshape(32, 128, 3).transpose(1, 0, 2).reshape(128, 96)
    out["convw"] = np.ascontiguousarray(cwt, np.float32)
    out["convb"] = np.ascontiguousarray(
        np.asarray(g["L0_ffn_conv_b"]).reshape(32, 128).T, np.float32)
    out.update(host_constants())
    return out


_CACHE = {}


def kernel(**inputs):
    if "prog" not in _CACHE:
        _CACHE["prog"] = build()
    nc, _ = _CACHE["prog"]
    shared = host_inputs(inputs)
    x = np.ascontiguousarray(np.asarray(inputs["L0_attn_a_x"]), np.float32)
    in_maps = []
    for c in range(8):
        m = dict(shared)
        m["x"] = np.ascontiguousarray(x[c])
        in_maps.append(m)
    res = run_bass_kernel_spmd(nc, in_maps, core_ids=list(range(8)))
    out = np.stack([np.asarray(r["out"]) for r in res.results], axis=0)
    return out.astype(np.float32)
